# version 26
# baseline (speedup 1.0000x reference)
import numpy as np
import ml_dtypes
import concourse.bass as bass
import concourse.mybir as mybir
import bass_rust
from concourse.bass_utils import run_bass_kernel_spmd

F32 = mybir.dt.float32
BF16 = mybir.dt.bfloat16
AF = mybir.ActivationFunctionType
ALU = mybir.AluOpType
AX = mybir.AxisListType
NPBF = ml_dtypes.bfloat16

NCORES = 8
D = 1024
S = 16
LCORE = 2048
C = 128
NCH = 16
DT = 8
TT = 16
CLIP_HI = float(np.float32(np.float32(1.0) - np.float32(1e-8)))


def _dram_ap(t, off, dims):
    ap = t.ap() if hasattr(t, "ap") and callable(getattr(t, "ap")) else t
    ap = ap.rearrange("a b -> (a b)").copy()
    ap.ap = bass_rust.VecI64Pair([[s, n] for (s, n) in dims])
    ap.offset = off
    return ap


class _Auto:

    def __init__(self, E, semh, start):
        self.E = E
        self.semh = semh
        self.n = start

    def wait_ge(self, s, v):
        self.E.wait_ge(s, v)

    def __getattr__(self, name):
        f = getattr(self.E, name)
        if not callable(f):
            return f

        def w(*a, **k):
            if self.n:
                self.E.wait_ge(self.semh, self.n)
            inst = f(*a, **k)
            inst.then_inc(self.semh, 1)
            self.n += 1
            return inst
        return w


class Prog:

    ENGS = ("t", "v", "a", "g", "s")
    AUTOSEM = {"v": "sV", "a": "sA", "g": "sP"}

    def __init__(self):
        self.q = {k: [] for k in self.ENGS}
        self.cnt = {}

    def n(self, sem):
        return self.cnt.get(sem, 0)

    def _bump(self, sem, k):
        self.cnt[sem] = self.cnt.get(sem, 0) + k
        return self.cnt[sem]

    def op(self, eng, fn, waits=(), incs=()):
        self.q[eng].append(("raw", fn, tuple(waits), tuple(incs)))
        r = None
        for s, k in incs:
            r = self._bump(s, k)
        return r

    def auto(self, eng, fn, waits=(), n=1):
        sem = self.AUTOSEM[eng]
        self.q[eng].append(("auto", fn, tuple(waits), (sem, n)))
        return self._bump(sem, n)

    def dma(self, eng, out, in_, waits=(), sem="sG", **kw):
        self.op(eng, lambda E, out=out, in_=in_, kw=kw: E.dma_start(out=out, in_=in_, **kw),
                waits=waits, incs=((sem, 16),))
        return (sem, self.cnt[sem])

    def emit(self, blk, sems):
        engmap = {"t": blk.tensor, "v": blk.vector, "a": blk.scalar,
                  "g": blk.gpsimd, "s": blk.sync}
        run = {}
        for eng in self.ENGS:
            lst = self.q[eng]
            if not lst:
                continue

            def body(E, lst=lst):
                for kind, fn, waits, extra in lst:
                    for s, v in waits:
                        E.wait_ge(sems[s], v)
                    if kind == "auto":
                        sem, n = extra
                        px = _Auto(E, sems[sem], run.get(sem, 0))
                        fn(px)
                        got = px.n - run.get(sem, 0)
                        assert got == n, f"auto op emitted {got} != declared {n}"
                        run[sem] = px.n
                    else:
                        inst = fn(E)
                        for s, k in extra:
                            inst.then_inc(sems[s], k)
                            if s in run or s in ("sV", "sA", "sP"):
                                run[s] = run.get(s, 0) + k
            engmap[eng](body)


def build_nc():
    nc = bass.Bass("TRN2", target_bir_lowering=False, debug=False, num_devices=NCORES)

    xin = nc.dram_tensor("xin", [LCORE, D], F32, kind="ExternalInput").ap()
    WinT = nc.dram_tensor("WinT", [1024, 2048], F32, kind="ExternalInput").ap()
    WBT = nc.dram_tensor("WBT", [1024, 16], F32, kind="ExternalInput").ap()
    WCT = nc.dram_tensor("WCT", [16, 1024], F32, kind="ExternalInput").ap()
    WoutT = nc.dram_tensor("WoutT", [1024, 1024], F32, kind="ExternalInput").ap()
    U1d = nc.dram_tensor("U1", [1024, 256], F32, kind="ExternalInput").ap()
    V1d = nc.dram_tensor("V1", [256, 4096], F32, kind="ExternalInput").ap()
    U2d = nc.dram_tensor("U2", [4096, 256], F32, kind="ExternalInput").ap()
    V2d = nc.dram_tensor("V2", [256, 1024], F32, kind="ExternalInput").ap()
    smalls_d = nc.dram_tensor("smalls", [96, 128], F32, kind="ExternalInput").ap()
    Alog_d = nc.dram_tensor("Alog", [1024, 16], F32, kind="ExternalInput").ap()
    brow_d = nc.dram_tensor("brow", [1, 2048], F32, kind="ExternalInput").ap()
    identF_d = nc.dram_tensor("identF", [128, 128], F32, kind="ExternalInput").ap()
    identB_d = nc.dram_tensor("identB", [128, 128], BF16, kind="ExternalInput").ap()
    Jbf_d = nc.dram_tensor("Jbf", [128, 128], BF16, kind="ExternalInput").ap()
    onesrow_d = nc.dram_tensor("onesrow", [1, 128], BF16, kind="ExternalInput").ap()
    crow_d = nc.dram_tensor("crow", [1, 256], F32, kind="ExternalInput").ap()
    ccol_d = nc.dram_tensor("ccol", [128, 1], F32, kind="ExternalInput").ap()
    conesb_d = nc.dram_tensor("conesb", [128, 2], BF16, kind="ExternalInput").ap()
    gmask_d = nc.dram_tensor("gmask", [128, 16], F32, kind="ExternalInput").ap()

    yout = nc.dram_tensor("y", [LCORE, D], F32, kind="ExternalOutput").ap()

    r_hbm = nc.dram_tensor("r_hbm", [LCORE, D], F32, kind="Internal")
    xp_hbm = nc.dram_tensor("xp_hbm", [1024, LCORE], BF16, kind="Internal")
    z_hbm = nc.dram_tensor("z_hbm", [1024, LCORE], BF16, kind="Internal")
    x1_hbm = nc.dram_tensor("x1_hbm", [LCORE, D], F32, kind="Internal")
    uh_hbm = nc.dram_tensor("uh_hbm", [16, 256], F32, kind="Internal")
    xm_hbm = nc.dram_tensor("xm_hbm", [256, 128], BF16, kind="Internal")
    logat_hbm = nc.dram_tensor("logat_hbm", [16, 1024], F32, kind="Internal")
    cc_in = nc.dram_tensor("cc_in", [128, 128], F32, kind="Internal")
    cc_out = nc.dram_tensor("cc_out", [1024, 128], F32, kind="Internal")

    from contextlib import ExitStack
    with ExitStack() as ctx:
        e = ctx.enter_context
        Txnh = e(nc.sbuf_tensor("Txnh", [128, 16384], BF16))
        Twb = e(nc.sbuf_tensor("Twb", [128, 16384], BF16))
        Ttby = e(nc.sbuf_tensor("Ttby", [128, 16384], BF16))
        Twb2 = e(nc.sbuf_tensor("Twb2", [128, 8192], BF16))
        Tscr = e(nc.sbuf_tensor("Tscr", [128, 7552], F32))
        Tio = e(nc.sbuf_tensor("Tio", [128, 2048], F32))
        Tob = e(nc.sbuf_tensor("Tob", [128, 2048], F32))

        identF = e(nc.sbuf_tensor("sb_identF", [128, 128], F32))
        identB = e(nc.sbuf_tensor("sb_identB", [128, 128], BF16))
        Jbf = e(nc.sbuf_tensor("sb_Jbf", [128, 128], BF16))
        onesrow = e(nc.sbuf_tensor("sb_onesrow", [1, 128], BF16))
        crow = e(nc.sbuf_tensor("sb_crow", [1, 256], F32))
        ccol = e(nc.sbuf_tensor("sb_ccol", [128, 1], F32))
        conesb = e(nc.sbuf_tensor("sb_conesb", [128, 2], BF16))
        gmask = e(nc.sbuf_tensor("sb_gmask", [128, 16], F32))
        smalls = e(nc.sbuf_tensor("sb_smalls", [96, 128], F32))
        smallsT = e(nc.sbuf_tensor("smallsT", [128, 96], F32))
        AlogT = e(nc.sbuf_tensor("AlogT", [128, 128], F32))
        logA = e(nc.sbuf_tensor("logA", [128, 128], F32))
        ST = e(nc.sbuf_tensor("ST", [128, 352], F32))
        RF = e(nc.sbuf_tensor("RF", [128, 160], F32))
        B_bf = e(nc.sbuf_tensor("B_bf", [16, 2048], BF16))
        Bt_bf = e(nc.sbuf_tensor("Bt_bf", [128, 256], BF16))
        intra = e(nc.sbuf_tensor("intra", [128, 2048], BF16))
        Tz = e(nc.sbuf_tensor("Tz", [128, 4096], BF16))
        xmean = e(nc.sbuf_tensor("xmean", [16, 2048], BF16))
        stag = e(nc.sbuf_tensor("stag", [128, 256], BF16))
        stagT = e(nc.sbuf_tensor("stagT", [128, 256], BF16, is_transpose_dest=True))
        WBTb = e(nc.sbuf_tensor("WBTb", [128, 128], BF16))
        WCTb = e(nc.sbuf_tensor("WCTb", [16, 1024], BF16))
        xnp = e(nc.sbuf_tensor("xnp", [128, 2048], BF16))
        Tstrip = e(nc.sbuf_tensor("Tstrip", [128, 4096], BF16))
        BRow = e(nc.sbuf_tensor("BRow", [1, 2048], F32))

        pbA = e(nc.psum_tensor("pbA", [128, 512], F32))
        pbB = e(nc.psum_tensor("pbB", [128, 512], F32))
        pbC = e(nc.psum_tensor("pbC", [128, 512], F32))
        pbD = e(nc.psum_tensor("pbD", [128, 512], F32))
        pbE = e(nc.psum_tensor("pbE", [128, 512], F32))
        pbF = e(nc.psum_tensor("pbF", [128, 512], F32))
        pbtA = e(nc.psum_tensor("pbtA", [128, 1024], BF16))
        pbtB = e(nc.psum_tensor("pbtB", [128, 1024], BF16))

        sems = {k: e(nc.semaphore(k)) for k in
                ("sG", "sXa", "sXb", "sOa", "sOb", "sRa", "sRb", "sSc",
                 "sCra", "sCrb", "sCxp", "sCza", "sCzb", "sC1a", "sC1b",
                 "sM4a", "sM4b", "sGoa", "sGob", "sYp", "sK", "sT", "sV", "sA", "sP", "sCC")}
        blk = e(nc.Block())

        P = Prog()
        W = lambda *pairs: tuple(pairs)

        SS0, ZA0, LI0, RA0 = 0, 2048, 2048 + 2176, 2048 + 2176 + 2176
        HC0, TH0 = RA0 + 896, RA0 + 896 + 128
        assert TH0 + 128 <= 7552
        MV0, MV3 = 0, 32
        BN6 = 64
        CVE, CLN, CRSTD, CVER, CV2, CV2E, CRS2 = 80, 96, 112, 128, 144, 160, 176
        CSComb, CC1R, CC0R, CNMU, CRS3 = 192, 208, 224, 240, 256
        CDT, CDTS, CDTM1, CDTM16 = 272, 280, 281, 282
        CACC, CAR = 288, 320
        CV3E = 336

        xn3 = Txnh[:].rearrange("p (a b) -> p a b", b=2048)
        sc1 = AlogT
        TzF = Tz[:].bitcast(F32)
        TzI = Tz[:].bitcast(mybir.dt.int32)
        RI = TzI[:, 1888:2048]

        xslot = [Tio[:, 0:1024], Tio[:, 1024:2048],
                 TzF[:, 0:1024], TzF[:, 1024:2048]]
        xnpslot = [xnp[:, 0:1024], xnp[:, 1024:2048],
                   Tstrip[:, 0:1024], Tstrip[:, 1024:2048]]
        Qbuf = xnp[:, 0:1024]
        zrow = smalls[0:16, :]
        TstripF = Tstrip[:].bitcast(F32)
        EST = TstripF[:, 1024:2048]
        ESTB = intra[:].bitcast(F32)
        TobB = Tob[:].bitcast(BF16)
        M4F = Tob[:]
        MbufB = TobB[:, 512:768]

        for dst, src in ((crow, crow_d), (BRow, brow_d), (identF, identF_d),
                         (ccol, ccol_d), (smalls, smalls_d),
                         (identB, identB_d), (Jbf, Jbf_d)):
            P.dma("g", dst[:], src, sem="sK")
        for dt_ in range(DT):
            P.dma("g", AlogT[:, dt_ * 16:(dt_ + 1) * 16],
                  Alog_d[dt_ * 128:(dt_ + 1) * 128, :], sem="sK")
        mK_pre = ("sK", P.n("sK"))
        mG_brow = mK_pre
        for dst, src in ((onesrow, onesrow_d), (conesb, conesb_d),
                         (gmask, gmask_d)):
            P.dma("g", dst[:], src)
        P.dma("g", WBTb[:].rearrange("p (k s) -> p k s", s=16),
              _dram_ap(WBT, 0, [(16, 128), (2048, 8), (1, 16)]))
        P.dma("g", WCTb[:, :], WCT)
        for k in range(8):
            P.dma("g", Twb[:, k * 2048:(k + 1) * 2048], WinT[k * 128:(k + 1) * 128, :])
        mG_p0 = ("sG", P.n("sG"))

        P.auto("g", lambda E: E.iota(RI[:, 0:128], pattern=[[1, 128]], base=1,
                                     channel_multiplier=0), n=1)
        P.auto("g", lambda E: E.iota(RI[:, 128:145], pattern=[[128, 17]], base=0,
                                     channel_multiplier=0), n=1)
        mP_iota = P.n("sP")

        P.op("t", lambda E: E.matmul(pbE[:, 0:96], smalls[:, :], identF[0:96, 0:96],
                                     start=True, stop=True),
             waits=W(mK_pre), incs=(("sT", 1),))
        mT_sm = P.n("sT")
        P.auto("v", lambda E: E.tensor_copy(smallsT[:], pbE[:, 0:96]),
               waits=W(("sT", mT_sm)), n=1)
        mV_smT = P.n("sV")

        P.auto("g", lambda E: E.iota(RI[:, 145:146], pattern=[[0, 1]], base=127,
                                     channel_multiplier=-1), n=1)
        mP_iota2 = P.n("sP")
        P.auto("v", lambda E: E.tensor_copy(RF[:, 0:145], RI[:, 0:145]),
               waits=W(("sP", mP_iota)), n=1)
        CRV = 285
        P.auto("v", lambda E: E.tensor_copy(ST[:, CRV:CRV + 1], RI[:, 145:146]),
               waits=W(("sP", mP_iota2)), n=1)
        mV_ramp = P.n("sV")
        CMH = 240
        P.auto("v", lambda E: E.memset(ST[:, CMH:CMH + 16], -0.5), n=1)
        mV_mh = P.n("sV")

        P.auto("a", lambda E: E.activation(ST[:, CDT:CDT + 8], smallsT[:, 88:96], AF.Exp),
               waits=W(("sV", mV_smT)), n=1)
        P.auto("v", lambda E: E.tensor_scalar(ST[:, CDT:CDT + 8], ST[:, CDT:CDT + 8],
                                              1e-4, 1.0, ALU.max, ALU.min),
               waits=W(("sA", P.n("sA"))), n=1)
        P.auto("v", lambda E: E.reduce_sum(ST[:, CDTS:CDTS + 1], ST[:, CDT:CDT + 8],
                                           axis=AX.X), n=1)
        P.op("t", lambda E: E.matmul(pbF[0:1, 0:1], ccol[:, 0:1], ST[:, CDTS:CDTS + 1],
                                     start=True, stop=True),
             waits=W(("sV", P.n("sV"))), incs=(("sT", 1),))
        P.auto("v", lambda E: E.tensor_copy(ST[0:1, CDTM1:CDTM1 + 1], pbF[0:1, 0:1]),
               waits=W(("sT", P.n("sT"))), n=1)
        P.op("t", lambda E: E.matmul(pbF[0:16, 1:2], crow[0:1, 128:144],
                                     ST[0:1, CDTM1:CDTM1 + 1], start=True, stop=True),
             waits=W(("sV", P.n("sV")), mG_brow), incs=(("sT", 1),))
        P.auto("v", lambda E: E.tensor_scalar(ST[0:16, CDTM16:CDTM16 + 1], pbF[0:16, 1:2],
                                              1.0 / 1024.0, None, ALU.mult, ALU.bypass),
               waits=W(("sT", P.n("sT"))), n=1)
        mV_dtm = P.n("sV")

        P.auto("a", lambda E: E.activation(sc1[:], AlogT[:], AF.Exp),
               waits=W(mK_pre), n=1)
        mA_ea = P.n("sA")

        def _u(E):
            for dt_ in range(DT):
                ins = E.tensor_scalar(sc1[:, dt_ * 16:(dt_ + 1) * 16],
                                      sc1[:, dt_ * 16:(dt_ + 1) * 16],
                                      ST[:, CDT + dt_:CDT + dt_ + 1], None,
                                      ALU.mult, ALU.bypass)
            return ins
        P.auto("v", _u, waits=W(("sA", mA_ea), ("sV", mV_dtm)), n=8)
        P.auto("a", lambda E: E.activation(sc1[:], sc1[:], AF.Exp, scale=-1.0),
               waits=W(("sV", P.n("sV"))), n=1)
        P.auto("v", lambda E: E.tensor_scalar(sc1[:], sc1[:], 1e-8, CLIP_HI,
                                              ALU.max, ALU.min),
               waits=W(("sA", P.n("sA"))), n=1)
        P.auto("a", lambda E: E.activation(logA[:], sc1[:], AF.Ln),
               waits=W(("sV", P.n("sV"))), n=1)
        mA_logA = P.n("sA")

        def _lam(E):
            for j in range(17):
                ins = E.activation(Tscr[:, LI0 + j * 128:LI0 + (j + 1) * 128],
                                   logA[:], AF.Exp, scale=float(128 * j))
            return ins
        P.auto("a", _lam, n=17)
        mA_lamI = P.n("sA")

        logATs = TstripF[0:16, 1024:2048]
        m_prept = []
        for rnd in range(2):
            bank = [pbC, pbD][rnd]

            def _ltr(E, rnd=rnd, bank=bank):
                for q in range(4):
                    dt_ = rnd * 4 + q
                    ins = E.matmul(bank[0:16, q * 128:(q + 1) * 128],
                                   logA[:, dt_ * 16:(dt_ + 1) * 16],
                                   identF[:, 0:128], start=True, stop=True)
                return ins
            P.op("t", _ltr, waits=W(("sA", mA_logA)), incs=(("sT", 1),))
            m_prept.append(P.n("sT"))
            P.auto("v", lambda E, rnd=rnd, bank=bank:
                   E.tensor_copy(logATs[:, rnd * 512:(rnd + 1) * 512],
                                 bank[0:16, 0:512]),
                   waits=W(("sT", m_prept[-1])), n=1)
        mV_prepev = P.n("sV")
        mC_lat = P.dma("a", logat_hbm.ap(), logATs, waits=W(("sV", mV_prepev)),
                       sem="sCxp")

        P.auto("v", lambda E: E.memset(zrow[:], 0.0), waits=W(("sT", mT_sm)), n=1)
        mV_zr = P.n("sV")
        P.dma("a", _dram_ap(uh_hbm, 128, [(256, 16), (1, 128)]), zrow[:],
              waits=W(("sV", mV_zr), mC_lat), sem="sCxp")
        mC_uh0 = ("sCxp", P.n("sCxp"))

        mA_exp = []
        mT_row = []
        mV_st1 = []
        mC_uhs = []
        mV_m4 = []
        mV_m4cast = []
        mT_m4 = []
        mS_xl = []
        mV_rw = []
        mC_rs = []
        mT_xtr = []
        mA_xev = []
        for i in range(TT):
            s_ = i

            slot = i % 4
            wts = []
            if i >= 4:
                wts.append(("sV", mV_st1[i - 4]))
            if slot >= 2:
                wts.append(("sV", mV_ramp))
            mS_xl.append(P.dma("s", xslot[slot],
                               xin[i * 128:(i + 1) * 128, :], waits=tuple(wts),
                               sem=["sXa", "sXb", "sOa", "sOb"][slot]))
            xt = xslot[slot]

            def _bn(E, i=i, xt=xt):
                E.bn_stats(ST[:, BN6:BN6 + 6], xt[:, 0:512])
                E.bn_stats(ST[:, BN6 + 6:BN6 + 12], xt[:, 512:1024])
                return E.bn_aggr(ST[:, MV0 + 2 * i:MV0 + 2 * i + 2], ST[:, BN6:BN6 + 12])
            P.auto("v", _bn, waits=W(mS_xl[i]), n=3)
            mV_st1.append(P.n("sV"))

        mu_all = ST[:, MV0:MV0 + 32:2]
        var_all = ST[:, MV0 + 1:MV0 + 32:2]

        def _lnsc(E):
            E.tensor_scalar(ST[:, CVE:CVE + 16], var_all, 1e-5, None,
                            ALU.add, ALU.bypass)
            return E.tensor_scalar(ST[:, CV2E:CV2E + 16], var_all,
                                   float(np.float32(1.0) + np.float32(1e-5)),
                                   1e-10, ALU.mult, ALU.add)
        P.auto("v", _lnsc, waits=W(("sV", mV_st1[-1])), n=2)
        mV_ln1 = P.n("sV")

        def _lnpw(E):
            E.tensor_tensor(ST[:, CRSTD:CRSTD + 16], ST[:, CVE:CVE + 16],
                            ST[:, CMH:CMH + 16], ALU.pow)
            return E.tensor_tensor(ST[:, CSComb:CSComb + 16], ST[:, CV2E:CV2E + 16],
                                   ST[:, CMH:CMH + 16], ALU.pow)
        P.auto("g", _lnpw, waits=W(("sV", mV_ln1), ("sV", mV_mh)), n=2)
        mP_lnpw = P.n("sP")

        def _lnsc3(E):
            E.tensor_scalar(ST[:, CC1R:CC1R + 16], ST[:, CRSTD:CRSTD + 16], 1.0,
                            None, ALU.add, ALU.bypass)
            return E.tensor_tensor(ST[:, CC0R:CC0R + 16], mu_all,
                                   ST[:, CRSTD:CRSTD + 16], ALU.mult)
        P.auto("v", _lnsc3, waits=W(("sP", mP_lnpw)), n=2)
        mV_lnsc = P.n("sV")

        mrow_l = []

        def _emit_bcast(s_, wts):
            mrow_l.append(P.dma("a", EST if s_ % 2 == 0 else ESTB,
                                _dram_ap(logat_hbm, s_ * 1024, [(0, 128), (1, 1024)]),
                                waits=tuple(wts), sem=["sCza", "sCzb"][s_ % 2]))

        def _emit_eexp(s_):
            dst = EST if s_ % 2 == 0 else ESTB

            def _eexp(E, s_=s_, dst=dst):
                E.activation(Ttby[:, s_ * 1024:s_ * 1024 + 512], dst[:, 0:512],
                             AF.Exp, scale=ST[:, CRV:CRV + 1],
                             accum_out=ST[:, CACC + 2 * s_:CACC + 2 * s_ + 1])
                return E.activation(Ttby[:, s_ * 1024 + 512:(s_ + 1) * 1024],
                                    dst[:, 512:1024], AF.Exp, scale=ST[:, CRV:CRV + 1],
                                    accum_out=ST[:, CACC + 2 * s_ + 1:CACC + 2 * s_ + 2])
            P.auto("a", _eexp, waits=W(mrow_l[s_], ("sV", mV_ramp)), n=2)
            mA_exp.append(P.n("sA"))

        _emit_bcast(0, [mC_lat])
        _emit_bcast(1, [])

        for i in range(TT):
            slot = i % 4
            rslot = i % 2
            _emit_eexp(i)
            if i + 2 < TT:
                _emit_bcast(i + 2, [("sA", mA_exp[i])])
            wts = [("sV", mV_st1[-1])]
            if i >= 4:
                wts.append(("sV", mV_rw[i - 4]))
            mS_x2 = P.dma("s", xslot[slot],
                          xin[i * 128:(i + 1) * 128, :], waits=tuple(wts),
                          sem=["sXa", "sXb", "sOa", "sOb"][slot])
            xt = xslot[slot]
            wts = [mS_x2, ("sV", mV_lnsc)]
            if i >= 4:
                wts.append(("sT", mT_xtr[i - 4]))
            P.auto("v", lambda E, i=i, xt=xt, slot=slot:
                   E.tensor_scalar(xnpslot[slot], xt,
                                   ST[:, MV0 + 2 * i:MV0 + 2 * i + 1],
                                   ST[:, CSComb + i:CSComb + i + 1],
                                   ALU.subtract, ALU.mult),
                   waits=tuple(wts), n=1)
            mV_xnp = P.n("sV")
            wts = []
            if i >= 2:
                wts.append(mC_rs[i - 2])
            P.auto("v", lambda E, i=i, xt=xt, rslot=rslot:
                   E.tensor_scalar(Tob[:, rslot * 1024:(rslot + 1) * 1024], xt,
                                   ST[:, CC1R + i:CC1R + i + 1],
                                   ST[:, CC0R + i:CC0R + i + 1],
                                   ALU.mult, ALU.subtract),
                   waits=tuple(wts), n=1)
            mV_rw.append(P.n("sV"))
            mC_rs.append(P.dma("g", r_hbm.ap()[i * 128:(i + 1) * 128, :],
                               Tob[:, rslot * 1024:(rslot + 1) * 1024],
                               waits=W(("sV", mV_rw[i])),
                               sem=["sCra", "sCrb"][rslot]))

            wts = [("sV", mV_xnp)]
            if i == 0:
                wts.append(mK_pre)
            if i >= 1:
                wts.append(("sA", mA_xev[i - 1][1]))
                wts.append(("sV", mA_xev[i - 1][2]))

            def _tp(E, i=i, slot=slot):
                for fb in range(8):
                    bk = pbtB if fb < 4 else pbtA
                    ins = E.transpose(bk[:, (fb % 4) * 128:(fb % 4 + 1) * 128],
                                      xnpslot[slot][:, fb * 128:(fb + 1) * 128],
                                      identB[:])
                return ins
            P.op("t", _tp, waits=tuple(wts), incs=(("sT", 1),))
            mT_xtr.append(P.n("sT"))

            def _xevA(E, i=i):
                for fb in range(4):
                    ins = E.activation(
                        Txnh[:, fb * 2048 + i * 128:fb * 2048 + (i + 1) * 128],
                        pbtB[:, fb * 128:(fb + 1) * 128], AF.Identity,
                        scale=smallsT[:, 72 + fb:73 + fb],
                        bias=smallsT[:, 80 + fb:81 + fb])
                return ins
            P.auto("a", _xevA, waits=W(("sT", mT_xtr[i]), ("sV", mV_smT)), n=4)
            mA_ev = P.n("sA")

            def _xevV(E, i=i):
                for fb in range(4, 8):
                    ins = E.tensor_scalar(
                        Txnh[:, fb * 2048 + i * 128:fb * 2048 + (i + 1) * 128],
                        pbtA[:, (fb - 4) * 128:(fb - 3) * 128],
                        smallsT[:, 72 + fb:73 + fb], smallsT[:, 80 + fb:81 + fb],
                        ALU.mult, ALU.add)
                return ins
            P.auto("v", _xevV, waits=W(("sT", mT_xtr[i]), ("sV", mV_smT)), n=4)
            mV_ev = P.n("sV")
            mA_xev.append(("both", mA_ev, mV_ev))
        mA_est = P.n("sA")
        mV_est = P.n("sV")
        mP_lnrw = P.n("sP")
        mA_xn_ts = [(("sA", mA_xev[4 * t + 3][1]), ("sV", mA_xev[4 * t + 3][2]))
                    for t in range(4)]

        def _arr(E):
            E.tensor_tensor(ST[:, CAR:CAR + 16], ST[:, CACC:CACC + 32:2],
                            ST[:, CACC + 1:CACC + 32:2], ALU.add)
            return E.tensor_scalar(ST[:, CAR:CAR + 16], ST[:, CAR:CAR + 16],
                                   1.0 / 1024.0, None, ALU.mult, ALU.bypass)
        P.auto("v", _arr, waits=W(("sA", mA_exp[-1])), n=2)
        mV_arr = P.n("sV")
        mC_uhs = []
        for s_ in range(16):
            P.dma("a", _dram_ap(uh_hbm, s_ * 256, [(1, 128), (1, 1)]),
                  ST[:, CAR + s_:CAR + s_ + 1],
                  waits=W(("sV", mV_arr)), sem="sCxp")
            mC_uhs.append(("sCxp", P.n("sCxp")))

        bank_free_a = {}
        mA_xpev = []
        mT_xp = []
        mC_xps = []
        mT_bp = []
        mV_bpev = []
        mM4w = []
        mT_m4 = []
        mA_m4 = []

        def _m4_step(g):
            if g % 2 == 0:
                s_, ph = g // 2, 0
            else:
                if g < 3:
                    return
                s_, ph = (g - 3) // 2, 1
            if s_ > 15:
                return
            if ph == 0:
                wts = [mC_uhs[-1]]
                if s_ < 2:
                    wts.extend([mC_rs[14], mC_rs[15]])
                if s_ >= 2:
                    wts.append(mA_m4[s_ - 2])
                mM4w.append(P.dma("a", M4F[:, (s_ % 2) * 128:(s_ % 2) * 128 + 128],
                                  _dram_ap(uh_hbm, s_ * 256, [(1, 128), (1, 128)]),
                                  waits=tuple(wts), sem=["sM4a", "sM4b"][s_ % 2]))
            else:
                P.auto("a", lambda E, s_=s_:
                       E.activation(MbufB[:, (s_ % 2) * 128:(s_ % 2) * 128 + 128],
                                    M4F[:, (s_ % 2) * 128:(s_ % 2) * 128 + 128],
                                    AF.Identity),
                       waits=W(mM4w[s_]), n=1)
                mcast = P.n("sA")
                wts = [("sA", mcast), ("sA", mA_est)]
                if s_ >= 2:
                    wts.append(("sT", mT_m4[s_ - 2]))
                bank = [pbE, pbF][s_ % 2]
                P.op("t", lambda E, s_=s_, bank=bank:
                     E.matmul(bank[:, 0:128],
                              MbufB[:, (s_ % 2) * 128:(s_ % 2) * 128 + 128],
                              Jbf[:], start=True, stop=True),
                     waits=tuple(wts), incs=(("sT", 1),))
                mT_m4.append(P.n("sT"))
                P.auto("a", lambda E, s_=s_, bank=bank:
                       E.activation(intra[:, s_ * 128:(s_ + 1) * 128],
                                    bank[:, 0:128], AF.Identity),
                       waits=W(("sT", mT_m4[-1])), n=1)
                mA_m4.append(("sA", P.n("sA")))

        mC_xps_round = []
        it = 0
        for ts_ in range(4):
            for ot in range(8):
                _m4_step(it)
                bank = [pbA, pbB][it % 2]
                wts = [mA_xn_ts[ts_][0], mA_xn_ts[ts_][1], mG_p0]
                if it >= 2:
                    wts.append(("sA", mA_xpev[it - 2]))

                def _mm(E, ot=ot, ts_=ts_, bank=bank):
                    for k in range(8):
                        ins = E.matmul(bank[:, :],
                                       Twb[:, k * 2048 + ot * 128:k * 2048 + (ot + 1) * 128],
                                       Txnh[:, k * 2048 + ts_ * 512:k * 2048 + (ts_ + 1) * 512],
                                       start=(k == 0), stop=(k == 7))
                    return ins
                P.op("t", _mm, waits=tuple(wts), incs=(("sT", 1),))
                mT_xp.append(P.n("sT"))

                wts = [("sT", mT_xp[-1]), ("sV", mV_smT)]
                if ts_ == 0:
                    if ot == 0:
                        wts.append(("sT", mT_xtr[14]))
                    elif ot == 2:
                        wts.append(("sT", mT_xtr[15]))
                    elif ot == 4:
                        wts.append(("sA", mA_est))
                if ts_ >= 1:
                    wts.append(("sT", mT_bp[(ts_ - 1) * 8 + 7]))
                    wts.append(mC_xps_round[ts_ - 1])
                P.auto("a", lambda E, ot=ot, bank=bank:
                     E.activation(Tstrip[:, ot * 512:(ot + 1) * 512], bank[:, :],
                                  AF.Identity, bias=smallsT[:, ot:ot + 1]),
                     waits=tuple(wts), n=1)
                mA_xpev.append(P.n("sA"))

                mC_xps.append(P.dma("a", xp_hbm.ap()[ot * 128:(ot + 1) * 128,
                                                     ts_ * 512:(ts_ + 1) * 512],
                                    Tstrip[:, ot * 512:(ot + 1) * 512],
                                    waits=W(("sA", mA_xpev[-1])), sem="sCxp"))
                it += 1

            wts = [("sA", mA_xpev[-1]), mG_p0]
            if ts_ >= 1:
                wts.append(("sV", mV_bpev[ts_ - 1]))

            def _bp(E, ts_=ts_):
                for k in range(8):
                    ins = E.matmul(pbC[0:16, :], WBTb[:, k * 16:(k + 1) * 16],
                                   Tstrip[:, k * 512:(k + 1) * 512],
                                   start=(k == 0), stop=(k == 7))
                return ins
            P.op("t", _bp, waits=tuple(wts), incs=(("sT", 1),))
            mT_bp.extend([P.n("sT")] * 8)
            P.auto("v", lambda E, ts_=ts_:
                 E.tensor_scalar(B_bf[:, ts_ * 512:(ts_ + 1) * 512], pbC[0:16, :],
                                 ST[0:16, CDTM16:CDTM16 + 1], None, ALU.mult, ALU.bypass),
                 waits=W(("sT", P.n("sT")), ("sV", mV_dtm)), n=1)
            mV_bpev.append(P.n("sV"))
            mC_xps_round.append(("sCxp", P.n("sCxp")))
        _m4_step(33)
        mV_bp = P.n("sV")

        def _btt(E):
            for i in range(16):
                ins = E.transpose(pbtA[:, i * 16:(i + 1) * 16],
                                  B_bf[0:16, i * 128:(i + 1) * 128], identB[0:16, 0:16])
            return ins
        P.op("t", _btt, waits=W(("sV", mV_bp), ("sA", mA_xev[TT - 1][1]),
                                ("sV", mA_xev[TT - 1][2])),
             incs=(("sT", 1),))
        mT_bt = P.n("sT")
        P.auto("v", lambda E: E.tensor_copy(Bt_bf[:], pbtA[:, 0:256]),
             waits=W(("sT", mT_bt)), n=1)
        mV_bt = P.n("sV")

        mV_ssev = []
        mT_cs = []
        for s_ in range(16):
            bank = [pbC, pbD][s_ % 2]
            wts = [("sV", mV_bt), ("sA", mA_est)]
            if s_ >= 2:
                wts.append(("sV", mV_ssev[s_ - 2]))

            def _cs(E, s_=s_, bank=bank):
                for dt_ in range(DT):
                    ins = E.matmul(bank[:, dt_ * 16:(dt_ + 1) * 16],
                                   Ttby[:, s_ * 1024 + dt_ * 128:s_ * 1024 + (dt_ + 1) * 128],
                                   Bt_bf[:, s_::16], start=True, stop=True)
                return ins
            P.op("t", _cs, waits=tuple(wts), incs=(("sT", 1),))
            mT_cs.append(P.n("sT"))
            P.auto("v", lambda E, s_=s_, bank=bank:
                 E.tensor_scalar(
                     Tscr[:, SS0:SS0 + 2048].rearrange("p (i r) -> p i r", r=128)
                     [:, :, s_:s_ + 113:16].rearrange("p i d -> p d i"),
                     bank[:, 0:128].rearrange("p (d i) -> p d i", i=16),
                     1.0 / 1024.0, None, ALU.mult, ALU.bypass),
                 waits=W(("sT", mT_cs[-1])), n=1)
            mV_ssev.append(P.n("sV"))
        mV_ss = P.n("sV")

        def _zit(E):
            E.memset(Tscr[:, ZA0:ZA0 + 128], 0.0)
            Lam = Tscr[:, LI0 + 128:LI0 + 256]
            for i in range(NCH):
                Zi = Tscr[:, ZA0 + i * 128:ZA0 + (i + 1) * 128]
                Zn = Tscr[:, ZA0 + (i + 1) * 128:ZA0 + (i + 2) * 128]
                Si = Tscr[:, SS0 + i * 128:SS0 + (i + 1) * 128]
                E.tensor_tensor(Zn, Zi, Lam, ALU.mult)
                ins = E.tensor_tensor(Zn, Zn, Si, ALU.add)
            return ins
        P.auto("v", _zit, waits=W(("sA", mA_lamI), ("sV", mV_ss), ("sP", mP_lnrw)), n=33)
        mV_z = P.n("sV")

        mT_csdone = P.n("sT")
        _pt_q = []
        for s_ in range(16):
            for dt_ in range(DT):
                _pt_q.append((s_, dt_))

        def _emit_pt(k):
            s_, dt_ = _pt_q[k]
            wts = [("sA", mA_lamI)] if k > 0 else [("sA", mA_lamI), ("sT", mT_csdone)]
            P.auto("a", lambda E, s_=s_, dt_=dt_:
                   E.activation(Ttby[:, s_ * 1024 + dt_ * 128:s_ * 1024 + (dt_ + 1) * 128],
                                RF[:, 0:128], AF.Exp,
                                scale=logA[:, dt_ * 16 + s_:dt_ * 16 + s_ + 1]),
                   waits=tuple(wts), n=1)

        mG_ccin = P.dma("g", cc_in.ap(), Tscr[:, ZA0 + 16 * 128:ZA0 + 17 * 128],
                        waits=W(("sV", mV_z)))
        P.op("g", lambda E: E.collective_compute(
            "AllGather", ALU.bypass, replica_groups=[list(range(NCORES))],
            ins=[cc_in.ap().opt()], outs=[cc_out.ap().opt()]),
            waits=W(mG_ccin), incs=(("sCC", 1),))
        mG_rall = P.dma("g", Tscr[:, RA0:RA0 + 896].rearrange("p (r c) -> p r c", c=128),
                        _dram_ap(cc_out, 0, [(128, 128), (16384, 7), (1, 128)]),
                        waits=W(("sCC", 1),))

        mA_zev = []
        mC_zs = []
        it = 0
        for ot in range(8):
            for ts_ in range(4):
                bank = [pbA, pbB][it % 2]
                wts = []
                if it >= 2:
                    wts.append(mA_zev[it - 2])
                else:
                    wts.append(("sA", mA_xpev[-1]))

                def _mmz(E, ot=ot, ts_=ts_, bank=bank):
                    for k in range(8):
                        ins = E.matmul(bank[:, :],
                                       Twb[:, k * 2048 + (8 + ot) * 128:k * 2048 + (9 + ot) * 128],
                                       Txnh[:, k * 2048 + ts_ * 512:k * 2048 + (ts_ + 1) * 512],
                                       start=(k == 0), stop=(k == 7))
                    return ins
                P.op("t", _mmz, waits=tuple(wts), incs=(("sT", 1),))
                mTz = P.n("sT")
                wts = [("sT", mTz)]
                if it < 4:
                    wts.append(mC_xps_round[3])
                if it >= 2:
                    wts.append(mC_zs[it - 2])
                zslot = it % 4
                P.auto("v", lambda E, zslot=zslot, ot=ot, bank=bank:
                     E.tensor_scalar(Tstrip[:, 2048 + zslot * 512:2048 + (zslot + 1) * 512],
                                     bank[:, :], smallsT[:, 8 + ot:9 + ot], None,
                                     ALU.add, ALU.bypass),
                     waits=tuple(wts), n=1)
                mA_zev.append(("sV", P.n("sV")))
                mC_zs.append(P.dma("a", z_hbm.ap()[ot * 128:(ot + 1) * 128,
                                                    ts_ * 512:(ts_ + 1) * 512],
                                   Tstrip[:, 2048 + zslot * 512:2048 + (zslot + 1) * 512],
                                   waits=W(mA_zev[-1]),
                                   sem=["sCza", "sCzb"][it % 2]))
                for k in range(4 * it, 4 * it + 4):
                    _emit_pt(k)
                it += 1
        mT_zdone = P.n("sT")
        mA_ptdone = P.n("sA")
        mC_zfin = (("sCza", P.n("sCza")), ("sCzb", P.n("sCzb")))

        def _hc(E):
            E.memset(Tscr[:, HC0:HC0 + 128], 0.0)
            Gam = Tscr[:, LI0 + 16 * 128:LI0 + 17 * 128]
            Hc = Tscr[:, HC0:HC0 + 128]
            tmp = Tscr[:, TH0:TH0 + 128]
            for c_ in range(7):
                E.tensor_scalar(tmp, Gam, gmask[:, c_:c_ + 1],
                                gmask[:, 8 + c_:9 + c_], ALU.mult, ALU.add)
                E.tensor_tensor(Hc, Hc, tmp, ALU.mult)
                E.tensor_scalar(tmp, Tscr[:, RA0 + c_ * 128:RA0 + (c_ + 1) * 128],
                                gmask[:, c_:c_ + 1], None, ALU.mult, ALU.bypass)
                ins = E.tensor_tensor(Hc, Hc, tmp, ALU.add)
            return ins
        P.auto("v", _hc, waits=W(mG_rall, ("sA", mA_lamI)), n=29)

        def _hin(E):
            Hc = Tscr[:, HC0:HC0 + 128]
            tmp = Tscr[:, TH0:TH0 + 128]
            for i in range(NCH):
                E.tensor_tensor(tmp, Tscr[:, LI0 + i * 128:LI0 + (i + 1) * 128],
                                Hc, ALU.mult)
                ins = E.tensor_tensor(Tz[:, i * 128:(i + 1) * 128], tmp,
                                      Tscr[:, ZA0 + i * 128:ZA0 + (i + 1) * 128],
                                      ALU.add)
            return ins
        P.auto("v", _hin, n=32)
        mV_hin = P.n("sV")

        mV_stag = []
        for s_ in range(16):
            bank = [pbC, pbD][s_ % 2]
            wts = [("sV", mV_hin), ("sA", mA_ptdone), mA_m4[-1]]
            if s_ >= 2:
                wts.append(("sV", mV_stag[s_ - 2]))

            def _ii(E, s_=s_, bank=bank):
                E.matmul(bank[:, 0:16], intra[:, s_ * 128:(s_ + 1) * 128],
                         Bt_bf[:, s_::16], start=True, stop=False)
                for dt_ in range(DT):
                    ins = E.matmul(bank[:, 0:16],
                                   Ttby[:, s_ * 1024 + dt_ * 128:s_ * 1024 + (dt_ + 1) * 128],
                                   Tz[:, dt_ * 16 + s_::128][:, 0:16],
                                   start=False, stop=(dt_ == DT - 1))
                return ins
            P.op("t", _ii, waits=tuple(wts), incs=(("sT", 1),))
            mTi = P.n("sT")
            P.auto("v", lambda E, s_=s_, bank=bank:
                 E.tensor_copy(stag[:, s_ * 16:(s_ + 1) * 16], bank[:, 0:16]),
                 waits=W(("sT", mTi)), n=1)
            mV_stag.append(P.n("sV"))
        mT_inter_done = P.n("sT")
        BRowb = intra
        P.auto("v", lambda E: E.tensor_copy(BRowb[0:1, 0:2048], BRow[0:1, :]),
               waits=W(("sT", mT_inter_done), mG_brow), n=1)
        mV_browb = P.n("sV")
        mV_stag_done = P.n("sV")

        P.dma("s", stagT[:, 0:128], stag[:, 0:128], waits=W(("sV", mV_stag_done)),
              transpose=True, sem="sSc")
        mS_st = P.dma("s", stagT[:, 128:256], stag[:, 128:256], transpose=True,
                      sem="sSc")
        P.dma("s", xm_hbm.ap()[0:128, :], stagT[:, 0:128], waits=W(mS_st), sem="sSc")
        mS_xm1 = P.dma("s", xm_hbm.ap()[128:256, :], stagT[:, 128:256], sem="sSc")
        mS_xmean = P.dma("s", xmean[:, :],
                         xm_hbm.ap().rearrange("(s r) c -> s (r c)", s=16),
                         waits=W(mS_xm1), sem="sSc")

        for k in range(8):
            P.dma("g", Twb2[:, k * 1024:(k + 1) * 1024],
                  WoutT[k * 128:(k + 1) * 128, :],
                  waits=W(mG_rall) if k == 0 else ())
        mG_wout = ("sG", P.n("sG"))

        mV_y_ts = []
        mV_ydone = []
        mS_rr = []
        mV_x1 = []
        mV_st3 = []
        mC_x1 = []
        mT_wo_tb = []
        ity = 0
        itw = 0
        for ts_ in range(4):
            sem = ["sRa", "sRb"][ts_ % 2]
            wts = []
            if ts_ == 0:
                wts = [mC_xps_round[3], ("sT", mT_inter_done)] + list(mC_zfin)
            else:
                wts = [("sV", mV_y_ts[ts_ - 1])]
            P.dma("s", Tstrip[:].rearrange("p (o c) -> p o c", c=512),
                  _dram_ap(xp_hbm, ts_ * 512,
                           [(2048, 128), (128 * 2048, 8), (1, 512)]),
                  waits=tuple(wts), sem=sem)
            P.dma("s", Tz[:].rearrange("p (o c) -> p o c", c=512),
                  _dram_ap(z_hbm, ts_ * 512,
                           [(2048, 128), (128 * 2048, 8), (1, 512)]),
                  sem=sem)
            mS_grp = (sem, P.n(sem))

            P.auto("a", lambda E: E.activation(Tz[:, 0:4096], Tz[:, 0:4096], AF.Silu),
                   waits=W(mS_grp), n=1)
            mAs = P.n("sA")
            for ot in range(8):
                bank = [pbE, pbF][ity % 2]
                wts = [mS_xmean, mG_p0, mS_grp]
                if ity >= 2:
                    wts.append(("sV", mV_ydone[ity - 2]))
                else:
                    wts.append(mA_m4[-1])
                P.op("t", lambda E, ot=ot, ts_=ts_, bank=bank:
                     E.matmul(bank[:, :], WCTb[:, ot * 128:(ot + 1) * 128],
                              xmean[:, ts_ * 512:(ts_ + 1) * 512],
                              start=True, stop=True),
                     waits=tuple(wts), incs=(("sT", 1),))
                mTy = P.n("sT")

                def _y(E, ot=ot, ts_=ts_, bank=bank):
                    E.scalar_tensor_tensor(
                        Qbuf[:, 0:512], Tstrip[:, ot * 512:(ot + 1) * 512],
                        smallsT[:, 64 + ot:65 + ot], bank[:, :], ALU.mult, ALU.add)
                    return E.tensor_tensor(
                        Ttby[:, ot * 2048 + ts_ * 512:ot * 2048 + (ts_ + 1) * 512],
                        Qbuf[:, 0:512], Tz[:, ot * 512:(ot + 1) * 512], ALU.mult)
                P.auto("v", _y, waits=W(("sT", mTy), ("sA", mAs)), n=2)
                mV_ydone.append(P.n("sV"))
                ity += 1
            mV_y_ts.append(P.n("sV"))

            for tb4 in range(4):
                tb = ts_ * 4 + tb4
                slot = tb % 2
                wts = [mC_rs[14], mC_rs[15]]
                if tb >= 2:
                    wts.append(("sV", mV_x1[(tb - 2) * 2 + 1]))
                else:
                    wts.append(("sV", mV_rw[TT - 1]))
                mS_rr.append(P.dma("s", Tio[:, slot * 1024:(slot + 1) * 1024],
                                   r_hbm.ap()[tb * 128:(tb + 1) * 128, :],
                                   waits=tuple(wts), sem=["sXa", "sXb"][slot]))
                for oh in range(2):
                    bank = [pbA, pbB][itw % 2]
                    wts = [("sV", mV_y_ts[ts_]), mG_wout]
                    if itw >= 2:
                        wts.append(("sV", mV_x1[itw - 2]))
                    else:
                        wts.append(mA_zev[-1])

                    def _wo(E, tb=tb, oh=oh, bank=bank):
                        E.matmul(bank[:, :], onesrow[0:1, :],
                                 BRowb[0:1, oh * 512:(oh + 1) * 512],
                                 start=True, stop=False)
                        for k in range(8):
                            ins = E.matmul(bank[:, :],
                                           Ttby[:, k * 2048 + tb * 128:k * 2048 + (tb + 1) * 128],
                                           Twb2[:, k * 1024 + oh * 512:k * 1024 + (oh + 1) * 512],
                                           start=False, stop=(k == 7))
                        return ins
                    if itw == 0:
                        wts.append(("sV", mV_browb))
                    P.op("t", _wo, waits=tuple(wts), incs=(("sT", 1),))
                    mTw = P.n("sT")
                    P.auto("v", lambda E, tb=tb, oh=oh, slot=slot, bank=bank:
                           E.tensor_tensor(Tob[:, slot * 1024 + oh * 512:slot * 1024 + (oh + 1) * 512],
                                           bank[:, :],
                                           Tio[:, slot * 1024 + oh * 512:slot * 1024 + (oh + 1) * 512],
                                           ALU.add),
                           waits=W(("sT", mTw), mS_rr[tb],
                                   (mC_x1[tb - 2] if tb >= 2 else mC_rs[15]),
                                   (mV_st3[tb - 2] if tb >= 2 else ("sA", 0))),
                           n=1)
                    mV_x1.append(P.n("sV"))
                    itw += 1
                mT_wo_tb.append(P.n("sT"))

                def _bn3(E, tb=tb, slot=slot):
                    ob = Tob[:, slot * 1024:(slot + 1) * 1024]
                    E.activation(Txnh[:, 0:1024], ob, AF.Identity,
                                 accum_out=ST[:, MV3 + 2 * tb:MV3 + 2 * tb + 1])
                    return E.activation(Txnh[:, 0:1024], ob, AF.Square,
                                        accum_out=ST[:, MV3 + 2 * tb + 1:MV3 + 2 * tb + 2])
                P.auto("a", _bn3, waits=W(("sV", mV_x1[-1])), n=2)
                mV_st3.append(("sA", P.n("sA")))
                mC_x1.append(P.dma("a", x1_hbm.ap()[tb * 128:(tb + 1) * 128, :],
                                   Tob[:, slot * 1024:(slot + 1) * 1024],
                                   waits=W(("sV", mV_x1[-1])),
                                   sem=["sC1a", "sC1b"][slot]))
        mT_woutdone = P.n("sT")
        mV_y = P.n("sV")

        mV_st3d = mV_st3[-1]
        def _ln3s(E):
            E.tensor_scalar(ST[:, MV3:MV3 + 32:2], ST[:, MV3:MV3 + 32:2],
                            1.0 / 1024.0, None, ALU.mult, ALU.bypass)
            E.tensor_tensor(ST[:, CLN:CLN + 16], ST[:, MV3:MV3 + 32:2],
                            ST[:, MV3:MV3 + 32:2], ALU.mult)
            E.tensor_scalar(ST[:, CV3E:CV3E + 16], ST[:, MV3 + 1:MV3 + 32:2],
                            1.0 / 1024.0, 1e-5, ALU.mult, ALU.add)
            return E.tensor_tensor(ST[:, CV3E:CV3E + 16], ST[:, CV3E:CV3E + 16],
                                   ST[:, CLN:CLN + 16], ALU.subtract)
        P.auto("v", _ln3s, waits=W(mV_st3d), n=4)
        P.auto("g", lambda E: E.tensor_tensor(ST[:, CRS3:CRS3 + 16],
                                              ST[:, CV3E:CV3E + 16],
                                              ST[:, CMH:CMH + 16], ALU.pow),
               waits=W(("sV", P.n("sV"))), n=1)
        mA_rs3 = ("sP", P.n("sP"))

        mS_l2, mV_n3, mT_t3, mV_e3 = [], [], [], []
        for i in range(TT):
            slot = i % 2
            wts = [mC_x1[i]]
            if i >= 2:
                wts.append(("sV", mV_n3[i - 2]))
            else:
                wts.append(("sV", mV_x1[-1]))
            mS_l2.append(P.dma("s", Tio[:, slot * 1024:(slot + 1) * 1024],
                               x1_hbm.ap()[i * 128:(i + 1) * 128, :],
                               waits=tuple(wts), sem=["sXa", "sXb"][slot]))
            wts = [mS_l2[i], mA_rs3, ("sT", mT_wo_tb[i])]
            if i >= 2:
                wts.append(("sT", mT_t3[i - 2]))
            P.auto("v", lambda E, i=i, slot=slot:
                 E.tensor_scalar(xnp[:, slot * 1024:(slot + 1) * 1024],
                                 Tio[:, slot * 1024:(slot + 1) * 1024],
                                 ST[:, MV3 + 2 * i:MV3 + 2 * i + 1],
                                 ST[:, CRS3 + i:CRS3 + i + 1],
                                 ALU.subtract, ALU.mult),
                 waits=tuple(wts), n=1)
            mV_n3.append(P.n("sV"))
            bank = [pbtA, pbtB][i % 2]
            wts = [("sV", mV_n3[i])]
            if i >= 2:
                wts.append(("sV", mV_e3[i - 2]))

            def _tp3(E, i=i, slot=slot, bank=bank):
                for fb in range(8):
                    ins = E.transpose(bank[:, fb * 128:(fb + 1) * 128],
                                      xnp[:, slot * 1024 + fb * 128:slot * 1024 + (fb + 1) * 128],
                                      identB[:])
                return ins
            P.op("t", _tp3, waits=tuple(wts), incs=(("sT", 1),))
            mT_t3.append(P.n("sT"))
            P.auto("v", lambda E, i=i, bank=bank:
                 E.tensor_copy(Ttby[:].rearrange("p (a b) -> p a b", b=2048)
                               [:, :, i * 128:(i + 1) * 128],
                               bank[:, 0:1024].rearrange("p (a b) -> p a b", b=128)),
                 waits=W(("sT", mT_t3[i])), n=1)
            mV_e3.append(P.n("sV"))
        mV_x1n = P.n("sV")

        for k in range(8):
            P.dma("g", Twb[:, k * 256:(k + 1) * 256], U1d[k * 128:(k + 1) * 128, :],
                  waits=W(("sT", mT_zdone), mG_wout) if k == 0 else ())
        for k in range(2):
            P.dma("g", Twb[:, 2048 + k * 4096:2048 + (k + 1) * 4096],
                  V1d[k * 128:(k + 1) * 128, :])
        for k in range(2):
            P.dma("g", Twb[:, 10240 + k * 1024:10240 + (k + 1) * 1024],
                  V2d[k * 128:(k + 1) * 128, :])
        mG_ffw1 = ("sG", P.n("sG"))
        for k in range(32):
            P.dma("g", Twb2[:, k * 256:(k + 1) * 256], U2d[k * 128:(k + 1) * 128, :],
                  waits=W(("sT", mT_woutdone), mG_ffw1) if k == 0 else ())
        mG_ffw2 = ("sG", P.n("sG"))

        PB0 = 12288
        mV_pev = []
        mA_hev = []
        mT_u2 = []
        mV_qev = []
        mV_oev = []
        mS_ol = []
        mS_os = []
        for st in range(4):
            for rt in range(2):
                k2 = st * 2 + rt
                bank = [pbC, pbD][rt]
                wts = [("sV", mV_e3[4 * st + 3]), mG_ffw1]
                if st >= 1:
                    wts.append(("sV", mV_qev[(st - 1) * 2 + rt]))

                def _u1(E, st=st, rt=rt, bank=bank):
                    for k in range(8):
                        ins = E.matmul(bank[:, :],
                                       Twb[:, k * 256 + rt * 128:k * 256 + (rt + 1) * 128],
                                       Ttby[:, k * 2048 + st * 512:k * 2048 + (st + 1) * 512],
                                       start=(k == 0), stop=(k == 7))
                    return ins
                P.op("t", _u1, waits=tuple(wts), incs=(("sT", 1),))
                mTp = P.n("sT")
                P.auto("v", lambda E, st=st, rt=rt, bank=bank:
                       E.tensor_copy(Twb[:, PB0 + rt * 2048 + st * 512:PB0 + rt * 2048 + (st + 1) * 512],
                                     bank[:, :]),
                       waits=W(("sT", mTp)), n=1)
                mV_pev.append(P.n("sV"))
            mV_pst = P.n("sV")

            def emit_u2(ot):
                wts = [("sA", mA_hev[st * 32 + ot])]
                if st >= 1 and ot == 0:
                    pass

                def _u2(E, st=st, ot=ot):
                    sl = (st * 32 + ot) % 3
                    E.matmul(pbC[:, :], Twb2[:, ot * 256:ot * 256 + 128],
                             Txnh[:, sl * 512:(sl + 1) * 512],
                             start=(ot == 0), stop=(ot == 31))
                    return E.matmul(pbD[:, :], Twb2[:, ot * 256 + 128:ot * 256 + 256],
                                    Txnh[:, sl * 512:(sl + 1) * 512],
                                    start=(ot == 0), stop=(ot == 31))
                P.op("t", _u2, waits=tuple(wts), incs=(("sT", 1),))
                mT_u2.append(P.n("sT"))

            for ot in range(32):
                g = st * 32 + ot
                bank = [pbA, pbB][ot % 2]
                wts = [("sV", mV_pst), mG_ffw2]
                if g >= 2:
                    wts.append(("sA", mA_hev[g - 2]))

                def _v1(E, st=st, ot=ot, bank=bank):
                    for k in range(2):
                        ins = E.matmul(bank[:, :],
                                       Twb[:, 2048 + k * 4096 + ot * 128:2048 + k * 4096 + (ot + 1) * 128],
                                       Twb[:, PB0 + k * 2048 + st * 512:PB0 + k * 2048 + (st + 1) * 512],
                                       start=(k == 0), stop=(k == 1))
                    return ins
                P.op("t", _v1, waits=tuple(wts), incs=(("sT", 1),))
                mTh = P.n("sT")
                wts = [("sT", mTh), ("sV", mV_smT)]
                if g >= 3:
                    wts.append(("sT", mT_u2[g - 3]))
                P.auto("a", lambda E, ot=ot, g=g, bank=bank:
                       E.activation(Txnh[:, (g % 3) * 512:(g % 3 + 1) * 512],
                                    bank[:, :], AF.Gelu_apprx_tanh,
                                    bias=smallsT[:, 24 + ot:25 + ot]),
                       waits=tuple(wts), n=1)
                mA_hev.append(P.n("sA"))
                if ot >= 2:
                    emit_u2(ot - 2)
            emit_u2(30)
            emit_u2(31)

            for rt2 in range(2):
                bank = [pbC, pbD][rt2]
                wts = [("sT", mT_u2[-1])]
                if st >= 1:
                    wts.append(("sT", mT_v2last))
                P.auto("v", lambda E, rt2=rt2, bank=bank:
                       E.tensor_copy(Qbuf[:, rt2 * 512:(rt2 + 1) * 512], bank[:, :]),
                       waits=tuple(wts), n=1)
                mV_qev.append(P.n("sV"))
            mV_qst = P.n("sV")

            for tb4 in range(4):
                tb = st * 4 + tb4
                slot = tb % 2
                wts = [mC_x1[tb]]
                if tb >= 2:
                    wts.append(("sV", mV_oev[(tb - 2) * 2 + 1]))
                else:
                    wts.append(("sV", mV_n3[TT - 1]))
                mS_ol.append(P.dma("s", Tio[:, slot * 1024:(slot + 1) * 1024],
                                   x1_hbm.ap()[tb * 128:(tb + 1) * 128, :],
                                   waits=tuple(wts), sem=["sXa", "sXb"][slot]))
                for dh in range(2):
                    bank = [pbE, pbF][dh]
                    wts = [("sV", mV_qst)]
                    if tb >= 1:
                        wts.append(("sV", mV_oev[(tb - 1) * 2 + dh]))

                    def _v2(E, tb4=tb4, dh=dh, bank=bank):
                        E.matmul(bank[:, :], onesrow[0:1, :],
                                 BRowb[0:1, 1024 + dh * 512:1024 + (dh + 1) * 512],
                                 start=True, stop=False)
                        for k in range(2):
                            ins = E.matmul(bank[:, :],
                                           Qbuf[:, k * 512 + tb4 * 128:k * 512 + (tb4 + 1) * 128],
                                           Twb[:, 10240 + k * 1024 + dh * 512:10240 + k * 1024 + (dh + 1) * 512],
                                           start=False, stop=(k == 1))
                        return ins
                    P.op("t", _v2, waits=tuple(wts), incs=(("sT", 1),))
                    mTo = P.n("sT")
                    wts = [("sT", mTo), mS_ol[tb]]
                    if tb >= 2:
                        wts.append(mS_os[tb - 2])
                    else:
                        wts.extend([mC_x1[14], mC_x1[15], mV_st3[-1]])
                    P.auto("v", lambda E, slot=slot, dh=dh, bank=bank:
                           E.tensor_tensor(
                               Tob[:, slot * 1024 + dh * 512:slot * 1024 + (dh + 1) * 512],
                               bank[:, :],
                               Tio[:, slot * 1024 + dh * 512:slot * 1024 + (dh + 1) * 512],
                               ALU.add),
                           waits=tuple(wts), n=1)
                    mV_oev.append(P.n("sV"))
                mS_os.append(P.dma("g", yout[tb * 128:(tb + 1) * 128, :],
                                   Tob[:, slot * 1024:(slot + 1) * 1024],
                                   waits=W(("sV", mV_oev[-1])),
                                   sem=["sGoa", "sGob"][slot]))
            mT_v2last = P.n("sT")

        allsems = ("sG", "sXa", "sXb", "sOa", "sOb", "sRa", "sRb", "sSc",
                   "sCra", "sCrb", "sCxp", "sCza", "sCzb", "sC1a", "sC1b",
                   "sM4a", "sM4b", "sGoa", "sGob", "sYp", "sK", "sT", "sV", "sA")
        P.op("g", lambda E: E.engine_nop(),
             waits=tuple((s, P.n(s)) for s in allsems),
             incs=(("sP", 1),))

        P.emit(blk, sems)

    return nc


_CACHE = {}


def _host_inputs(inputs):
    f32 = np.float32
    W_in = np.asarray(inputs["L0_attn_W_in"], f32)
    smalls = np.concatenate([
        np.asarray(inputs["L0_attn_b_in"], f32).reshape(16, 128),
        np.asarray(inputs["L0_attn_b_out"], f32).reshape(8, 128),
        np.asarray(inputs["L0_ffn_b1"], f32).reshape(32, 128),
        np.asarray(inputs["L0_ffn_b2"], f32).reshape(8, 128),
        np.asarray(inputs["L0_attn_D_skip"], f32).reshape(8, 128),
        np.asarray(inputs["L0_attn_ln_g"], f32).reshape(8, 128),
        np.asarray(inputs["L0_attn_ln_b"], f32).reshape(8, 128),
        np.asarray(inputs["L0_attn_log_dt"], f32).reshape(8, 128),
    ], axis=0)
    brow = np.concatenate([np.asarray(inputs["L0_attn_b_out"], f32),
                           np.asarray(inputs["L0_ffn_b2"], f32)]).reshape(1, 2048)
    crow = np.zeros((1, 256), f32)
    crow[0, 0:128] = np.arange(127, -1, -1, dtype=f32)
    crow[0, 128:256] = 1.0
    com = {
        "WinT": np.ascontiguousarray(W_in.T),
        "WBT": np.ascontiguousarray(np.asarray(inputs["L0_attn_W_B"], f32).T),
        "WCT": np.ascontiguousarray(np.asarray(inputs["L0_attn_W_C"], f32).T),
        "WoutT": np.ascontiguousarray(np.asarray(inputs["L0_attn_W_out"], f32).T),
        "U1": np.ascontiguousarray(np.asarray(inputs["L0_ffn_U1"], f32)),
        "V1": np.ascontiguousarray(np.asarray(inputs["L0_ffn_V1"], f32)),
        "U2": np.ascontiguousarray(np.asarray(inputs["L0_ffn_U2"], f32)),
        "V2": np.ascontiguousarray(np.asarray(inputs["L0_ffn_V2"], f32)),
        "smalls": smalls,
        "Alog": np.ascontiguousarray(np.asarray(inputs["L0_attn_A_log"], f32)),
        "brow": brow,
        "identF": np.eye(128, dtype=f32),
        "identB": np.eye(128, dtype=NPBF),
        "Jbf": np.eye(128, dtype=NPBF)[:, ::-1].copy(),
        "onesrow": np.ones((1, 128), NPBF),
        "crow": crow,
        "ccol": np.ones((128, 1), f32),
        "conesb": np.ones((128, 2), NPBF),
    }
    x = np.asarray(inputs["L0_attn_x"], f32).reshape(16384, 1024)
    in_maps = []
    for c in range(NCORES):
        m = dict(com)
        m["xin"] = np.ascontiguousarray(x[c * LCORE:(c + 1) * LCORE])
        gm = np.zeros((128, 16), f32)
        gm[:, 0:7] = (np.arange(7)[None, :] < c)
        gm[:, 8:15] = 1.0 - gm[:, 0:7]
        m["gmask"] = gm
        in_maps.append(m)
    return in_maps


def kernel(**inputs):
    if "nc" not in _CACHE:
        _CACHE["nc"] = build_nc()
    nc = _CACHE["nc"]
    in_maps = _host_inputs(inputs)
    res = run_bass_kernel_spmd(nc, in_maps, core_ids=list(range(NCORES)))
    out = np.concatenate([np.asarray(res.results[c]["y"]) for c in range(NCORES)],
                         axis=0)
    return out.reshape(1, 16384, 1024).astype(np.float32)


# version 27 (final)
# speedup vs baseline: 1.0279x; 1.0279x over previous
import numpy as np
import ml_dtypes
import concourse.bass as bass
import concourse.mybir as mybir
import bass_rust
from concourse.bass_utils import run_bass_kernel_spmd

F32 = mybir.dt.float32
BF16 = mybir.dt.bfloat16
AF = mybir.ActivationFunctionType
ALU = mybir.AluOpType
AX = mybir.AxisListType
NPBF = ml_dtypes.bfloat16

NCORES = 8
D = 1024
S = 16
LCORE = 2048
C = 128
NCH = 16
DT = 8
TT = 16
CLIP_HI = float(np.float32(np.float32(1.0) - np.float32(1e-8)))


def _dram_ap(t, off, dims):
    ap = t.ap() if hasattr(t, "ap") and callable(getattr(t, "ap")) else t
    ap = ap.rearrange("a b -> (a b)").copy()
    ap.ap = bass_rust.VecI64Pair([[s, n] for (s, n) in dims])
    ap.offset = off
    return ap


class _Auto:

    def __init__(self, E, semh, start):
        self.E = E
        self.semh = semh
        self.n = start

    def wait_ge(self, s, v):
        self.E.wait_ge(s, v)

    def __getattr__(self, name):
        f = getattr(self.E, name)
        if not callable(f):
            return f

        def w(*a, **k):
            if self.n:
                self.E.wait_ge(self.semh, self.n)
            inst = f(*a, **k)
            inst.then_inc(self.semh, 1)
            self.n += 1
            return inst
        return w


class Prog:

    ENGS = ("t", "v", "a", "g", "s")
    AUTOSEM = {"v": "sV", "a": "sA", "g": "sP"}

    def __init__(self):
        self.q = {k: [] for k in self.ENGS}
        self.cnt = {}

    def n(self, sem):
        return self.cnt.get(sem, 0)

    def _bump(self, sem, k):
        self.cnt[sem] = self.cnt.get(sem, 0) + k
        return self.cnt[sem]

    def op(self, eng, fn, waits=(), incs=()):
        self.q[eng].append(("raw", fn, tuple(waits), tuple(incs)))
        r = None
        for s, k in incs:
            r = self._bump(s, k)
        return r

    def auto(self, eng, fn, waits=(), n=1):
        sem = self.AUTOSEM[eng]
        self.q[eng].append(("auto", fn, tuple(waits), (sem, n)))
        return self._bump(sem, n)

    def dma(self, eng, out, in_, waits=(), sem="sG", **kw):
        self.op(eng, lambda E, out=out, in_=in_, kw=kw: E.dma_start(out=out, in_=in_, **kw),
                waits=waits, incs=((sem, 16),))
        return (sem, self.cnt[sem])

    def emit(self, blk, sems):
        engmap = {"t": blk.tensor, "v": blk.vector, "a": blk.scalar,
                  "g": blk.gpsimd, "s": blk.sync}
        run = {}
        for eng in self.ENGS:
            lst = self.q[eng]
            if not lst:
                continue

            def body(E, lst=lst):
                for kind, fn, waits, extra in lst:
                    for s, v in waits:
                        E.wait_ge(sems[s], v)
                    if kind == "auto":
                        sem, n = extra
                        px = _Auto(E, sems[sem], run.get(sem, 0))
                        fn(px)
                        got = px.n - run.get(sem, 0)
                        assert got == n, f"auto op emitted {got} != declared {n}"
                        run[sem] = px.n
                    else:
                        inst = fn(E)
                        for s, k in extra:
                            inst.then_inc(sems[s], k)
                            if s in run or s in ("sV", "sA", "sP"):
                                run[s] = run.get(s, 0) + k
            engmap[eng](body)


def build_nc():
    nc = bass.Bass("TRN2", target_bir_lowering=False, debug=False, num_devices=NCORES)

    xin = nc.dram_tensor("xin", [LCORE, D], F32, kind="ExternalInput").ap()
    WinT = nc.dram_tensor("WinT", [1024, 2048], F32, kind="ExternalInput").ap()
    WBT = nc.dram_tensor("WBT", [1024, 16], F32, kind="ExternalInput").ap()
    WCT = nc.dram_tensor("WCT", [16, 1024], F32, kind="ExternalInput").ap()
    WoutT = nc.dram_tensor("WoutT", [1024, 1024], F32, kind="ExternalInput").ap()
    U1d = nc.dram_tensor("U1", [1024, 256], F32, kind="ExternalInput").ap()
    V1d = nc.dram_tensor("V1", [256, 4096], F32, kind="ExternalInput").ap()
    U2d = nc.dram_tensor("U2", [4096, 256], F32, kind="ExternalInput").ap()
    V2d = nc.dram_tensor("V2", [256, 1024], F32, kind="ExternalInput").ap()
    smalls_d = nc.dram_tensor("smalls", [96, 128], F32, kind="ExternalInput").ap()
    Alog_d = nc.dram_tensor("Alog", [1024, 16], F32, kind="ExternalInput").ap()
    brow_d = nc.dram_tensor("brow", [1, 2048], F32, kind="ExternalInput").ap()
    identF_d = nc.dram_tensor("identF", [128, 128], F32, kind="ExternalInput").ap()
    identB_d = nc.dram_tensor("identB", [128, 128], BF16, kind="ExternalInput").ap()
    Jbf_d = nc.dram_tensor("Jbf", [128, 128], BF16, kind="ExternalInput").ap()
    onesrow_d = nc.dram_tensor("onesrow", [1, 128], BF16, kind="ExternalInput").ap()
    crow_d = nc.dram_tensor("crow", [1, 256], F32, kind="ExternalInput").ap()
    ccol_d = nc.dram_tensor("ccol", [128, 1], F32, kind="ExternalInput").ap()
    conesb_d = nc.dram_tensor("conesb", [128, 2], BF16, kind="ExternalInput").ap()
    gmask_d = nc.dram_tensor("gmask", [128, 16], F32, kind="ExternalInput").ap()

    yout = nc.dram_tensor("y", [LCORE, D], F32, kind="ExternalOutput").ap()

    r_hbm = nc.dram_tensor("r_hbm", [LCORE, D], F32, kind="Internal")
    xp_hbm = nc.dram_tensor("xp_hbm", [1024, LCORE], BF16, kind="Internal")
    z_hbm = nc.dram_tensor("z_hbm", [1024, LCORE], BF16, kind="Internal")
    x1_hbm = nc.dram_tensor("x1_hbm", [LCORE, D], F32, kind="Internal")
    uh_hbm = nc.dram_tensor("uh_hbm", [16, 256], F32, kind="Internal")
    xm_hbm = nc.dram_tensor("xm_hbm", [256, 128], BF16, kind="Internal")
    logat_hbm = nc.dram_tensor("logat_hbm", [16, 1024], F32, kind="Internal")
    cc_in = nc.dram_tensor("cc_in", [128, 128], F32, kind="Internal")
    cc_out = nc.dram_tensor("cc_out", [1024, 128], F32, kind="Internal")

    from contextlib import ExitStack
    with ExitStack() as ctx:
        e = ctx.enter_context
        Txnh = e(nc.sbuf_tensor("Txnh", [128, 16384], BF16))
        Twb = e(nc.sbuf_tensor("Twb", [128, 16384], BF16))
        Ttby = e(nc.sbuf_tensor("Ttby", [128, 16384], BF16))
        Twb2 = e(nc.sbuf_tensor("Twb2", [128, 8192], BF16))
        Tscr = e(nc.sbuf_tensor("Tscr", [128, 7552], F32))
        Tio = e(nc.sbuf_tensor("Tio", [128, 2048], F32))
        Tob = e(nc.sbuf_tensor("Tob", [128, 2048], F32))

        identF = e(nc.sbuf_tensor("sb_identF", [128, 128], F32))
        identB = e(nc.sbuf_tensor("sb_identB", [128, 128], BF16))
        Jbf = e(nc.sbuf_tensor("sb_Jbf", [128, 128], BF16))
        onesrow = e(nc.sbuf_tensor("sb_onesrow", [1, 128], BF16))
        crow = e(nc.sbuf_tensor("sb_crow", [1, 256], F32))
        ccol = e(nc.sbuf_tensor("sb_ccol", [128, 1], F32))
        conesb = e(nc.sbuf_tensor("sb_conesb", [128, 2], BF16))
        gmask = e(nc.sbuf_tensor("sb_gmask", [128, 16], F32))
        smalls = e(nc.sbuf_tensor("sb_smalls", [96, 128], F32))
        smallsT = e(nc.sbuf_tensor("smallsT", [128, 96], F32))
        AlogT = e(nc.sbuf_tensor("AlogT", [128, 128], F32))
        logA = e(nc.sbuf_tensor("logA", [128, 128], F32))
        ST = e(nc.sbuf_tensor("ST", [128, 352], F32))
        RF = e(nc.sbuf_tensor("RF", [128, 160], F32))
        B_bf = e(nc.sbuf_tensor("B_bf", [16, 2048], BF16))
        Bt_bf = e(nc.sbuf_tensor("Bt_bf", [128, 256], BF16))
        intra = e(nc.sbuf_tensor("intra", [128, 2048], BF16))
        Tz = e(nc.sbuf_tensor("Tz", [128, 4096], BF16))
        xmean = e(nc.sbuf_tensor("xmean", [16, 2048], BF16))
        stag = e(nc.sbuf_tensor("stag", [128, 256], BF16))
        stagT = e(nc.sbuf_tensor("stagT", [128, 256], BF16, is_transpose_dest=True))
        WBTb = e(nc.sbuf_tensor("WBTb", [128, 128], BF16))
        WCTb = e(nc.sbuf_tensor("WCTb", [16, 1024], BF16))
        xnp = e(nc.sbuf_tensor("xnp", [128, 2048], BF16))
        Tstrip = e(nc.sbuf_tensor("Tstrip", [128, 4096], BF16))
        BRow = e(nc.sbuf_tensor("BRow", [1, 2048], F32))

        pbA = e(nc.psum_tensor("pbA", [128, 512], F32))
        pbB = e(nc.psum_tensor("pbB", [128, 512], F32))
        pbC = e(nc.psum_tensor("pbC", [128, 512], F32))
        pbD = e(nc.psum_tensor("pbD", [128, 512], F32))
        pbE = e(nc.psum_tensor("pbE", [128, 512], F32))
        pbF = e(nc.psum_tensor("pbF", [128, 512], F32))
        pbtA = e(nc.psum_tensor("pbtA", [128, 1024], BF16))
        pbtB = e(nc.psum_tensor("pbtB", [128, 1024], BF16))

        sems = {k: e(nc.semaphore(k)) for k in
                ("sG", "sXa", "sXb", "sOa", "sOb", "sRa", "sRb", "sSc",
                 "sCra", "sCrb", "sCxp", "sCza", "sCzb", "sC1a", "sC1b",
                 "sM4a", "sM4b", "sGoa", "sGob", "sYp", "sK", "sT", "sV", "sA", "sP", "sCC")}
        blk = e(nc.Block())

        P = Prog()
        W = lambda *pairs: tuple(pairs)

        SS0, ZA0, LI0, RA0 = 0, 2048, 2048 + 2176, 2048 + 2176 + 2176
        HC0, TH0 = RA0 + 896, RA0 + 896 + 128
        assert TH0 + 128 <= 7552
        MV0, MV3 = 0, 32
        BN6 = 64
        CVE, CLN, CRSTD, CVER, CV2, CV2E, CRS2 = 80, 96, 112, 128, 144, 160, 176
        CSComb, CC1R, CC0R, CNMU, CRS3 = 192, 208, 224, 240, 256
        CDT, CDTS, CDTM1, CDTM16 = 272, 280, 281, 282
        CACC, CAR = 288, 320
        CV3E = 336

        xn3 = Txnh[:].rearrange("p (a b) -> p a b", b=2048)
        sc1 = AlogT
        TzF = Tz[:].bitcast(F32)
        TzI = Tz[:].bitcast(mybir.dt.int32)
        RI = TzI[:, 1888:2048]

        xslot = [Tio[:, 0:1024], Tio[:, 1024:2048],
                 TzF[:, 0:1024], TzF[:, 1024:2048]]
        xnpslot = [xnp[:, 0:1024], xnp[:, 1024:2048],
                   Tstrip[:, 0:1024], Tstrip[:, 1024:2048]]
        Qbuf = xnp[:, 0:1024]
        zrow = smalls[0:16, :]
        TstripF = Tstrip[:].bitcast(F32)
        EST = TstripF[:, 1024:2048]
        ESTB = intra[:].bitcast(F32)
        TobB = Tob[:].bitcast(BF16)
        M4F = Tob[:]
        MbufB = TobB[:, 512:768]

        for dst, src in ((crow, crow_d), (BRow, brow_d), (identF, identF_d),
                         (ccol, ccol_d), (smalls, smalls_d),
                         (identB, identB_d), (Jbf, Jbf_d)):
            P.dma("g", dst[:], src, sem="sK")
        for dt_ in range(DT):
            P.dma("g", AlogT[:, dt_ * 16:(dt_ + 1) * 16],
                  Alog_d[dt_ * 128:(dt_ + 1) * 128, :], sem="sK")
        mK_pre = ("sK", P.n("sK"))
        mG_brow = mK_pre
        for dst, src in ((onesrow, onesrow_d), (conesb, conesb_d),
                         (gmask, gmask_d)):
            P.dma("g", dst[:], src)
        P.dma("g", WBTb[:].rearrange("p (k s) -> p k s", s=16),
              _dram_ap(WBT, 0, [(16, 128), (2048, 8), (1, 16)]))
        P.dma("g", WCTb[:, :], WCT)
        for k in range(8):
            P.dma("g", Twb[:, k * 2048:(k + 1) * 2048], WinT[k * 128:(k + 1) * 128, :])
        mG_p0 = ("sG", P.n("sG"))

        P.auto("g", lambda E: E.iota(RI[:, 0:128], pattern=[[1, 128]], base=1,
                                     channel_multiplier=0), n=1)
        P.auto("g", lambda E: E.iota(RI[:, 128:145], pattern=[[128, 17]], base=0,
                                     channel_multiplier=0), n=1)
        mP_iota = P.n("sP")

        P.op("t", lambda E: E.matmul(pbE[:, 0:96], smalls[:, :], identF[0:96, 0:96],
                                     start=True, stop=True),
             waits=W(mK_pre), incs=(("sT", 1),))
        mT_sm = P.n("sT")
        P.auto("v", lambda E: E.tensor_copy(smallsT[:], pbE[:, 0:96]),
               waits=W(("sT", mT_sm)), n=1)
        mV_smT = P.n("sV")

        P.auto("g", lambda E: E.iota(RI[:, 145:146], pattern=[[0, 1]], base=127,
                                     channel_multiplier=-1), n=1)
        mP_iota2 = P.n("sP")
        P.auto("v", lambda E: E.tensor_copy(RF[:, 0:145], RI[:, 0:145]),
               waits=W(("sP", mP_iota)), n=1)
        CRV = 285
        P.auto("v", lambda E: E.tensor_copy(ST[:, CRV:CRV + 1], RI[:, 145:146]),
               waits=W(("sP", mP_iota2)), n=1)
        mV_ramp = P.n("sV")
        CMH = 240
        P.auto("v", lambda E: E.memset(ST[:, CMH:CMH + 16], -0.5), n=1)
        mV_mh = P.n("sV")

        P.auto("a", lambda E: E.activation(ST[:, CDT:CDT + 8], smallsT[:, 88:96], AF.Exp),
               waits=W(("sV", mV_smT)), n=1)
        P.auto("v", lambda E: E.tensor_scalar(ST[:, CDT:CDT + 8], ST[:, CDT:CDT + 8],
                                              1e-4, 1.0, ALU.max, ALU.min),
               waits=W(("sA", P.n("sA"))), n=1)
        P.auto("v", lambda E: E.reduce_sum(ST[:, CDTS:CDTS + 1], ST[:, CDT:CDT + 8],
                                           axis=AX.X), n=1)
        P.op("t", lambda E: E.matmul(pbF[0:1, 0:1], ccol[:, 0:1], ST[:, CDTS:CDTS + 1],
                                     start=True, stop=True),
             waits=W(("sV", P.n("sV"))), incs=(("sT", 1),))
        P.auto("v", lambda E: E.tensor_copy(ST[0:1, CDTM1:CDTM1 + 1], pbF[0:1, 0:1]),
               waits=W(("sT", P.n("sT"))), n=1)
        P.op("t", lambda E: E.matmul(pbF[0:16, 1:2], crow[0:1, 128:144],
                                     ST[0:1, CDTM1:CDTM1 + 1], start=True, stop=True),
             waits=W(("sV", P.n("sV")), mG_brow), incs=(("sT", 1),))
        P.auto("v", lambda E: E.tensor_scalar(ST[0:16, CDTM16:CDTM16 + 1], pbF[0:16, 1:2],
                                              1.0 / 1024.0, None, ALU.mult, ALU.bypass),
               waits=W(("sT", P.n("sT"))), n=1)
        mV_dtm = P.n("sV")

        P.auto("a", lambda E: E.activation(sc1[:], AlogT[:], AF.Exp),
               waits=W(mK_pre), n=1)
        mA_ea = P.n("sA")

        def _u(E):
            for dt_ in range(DT):
                ins = E.tensor_scalar(sc1[:, dt_ * 16:(dt_ + 1) * 16],
                                      sc1[:, dt_ * 16:(dt_ + 1) * 16],
                                      ST[:, CDT + dt_:CDT + dt_ + 1], None,
                                      ALU.mult, ALU.bypass)
            return ins
        P.auto("v", _u, waits=W(("sA", mA_ea), ("sV", mV_dtm)), n=8)
        P.auto("a", lambda E: E.activation(sc1[:], sc1[:], AF.Exp, scale=-1.0),
               waits=W(("sV", P.n("sV"))), n=1)
        P.auto("v", lambda E: E.tensor_scalar(sc1[:], sc1[:], 1e-8, CLIP_HI,
                                              ALU.max, ALU.min),
               waits=W(("sA", P.n("sA"))), n=1)
        P.auto("a", lambda E: E.activation(logA[:], sc1[:], AF.Ln),
               waits=W(("sV", P.n("sV"))), n=1)
        mA_logA = P.n("sA")

        def _lam(E):
            for j in range(17):
                ins = E.activation(Tscr[:, LI0 + j * 128:LI0 + (j + 1) * 128],
                                   logA[:], AF.Exp, scale=float(128 * j))
            return ins
        P.auto("a", _lam, n=17)
        mA_lamI = P.n("sA")

        logATs = TstripF[0:16, 1024:2048]
        m_prept = []
        for rnd in range(2):
            bank = [pbC, pbD][rnd]

            def _ltr(E, rnd=rnd, bank=bank):
                for q in range(4):
                    dt_ = rnd * 4 + q
                    ins = E.matmul(bank[0:16, q * 128:(q + 1) * 128],
                                   logA[:, dt_ * 16:(dt_ + 1) * 16],
                                   identF[:, 0:128], start=True, stop=True)
                return ins
            P.op("t", _ltr, waits=W(("sA", mA_logA)), incs=(("sT", 1),))
            m_prept.append(P.n("sT"))
            P.auto("v", lambda E, rnd=rnd, bank=bank:
                   E.tensor_copy(logATs[:, rnd * 512:(rnd + 1) * 512],
                                 bank[0:16, 0:512]),
                   waits=W(("sT", m_prept[-1])), n=1)
        mV_prepev = P.n("sV")
        mC_lat = P.dma("a", logat_hbm.ap(), logATs, waits=W(("sV", mV_prepev)),
                       sem="sCxp")

        P.auto("v", lambda E: E.memset(zrow[:], 0.0), waits=W(("sT", mT_sm)), n=1)
        mV_zr = P.n("sV")
        P.dma("a", _dram_ap(uh_hbm, 128, [(256, 16), (1, 128)]), zrow[:],
              waits=W(("sV", mV_zr), mC_lat), sem="sCxp")
        mC_uh0 = ("sCxp", P.n("sCxp"))

        mA_exp = []
        mT_row = []
        mV_st1 = []
        mC_uhs = []
        mV_m4 = []
        mV_m4cast = []
        mT_m4 = []
        mS_xl = []
        mV_rw = []
        mC_rs = []
        mT_xtr = []
        mA_xev = []
        for i in range(TT):
            s_ = i

            slot = i % 4
            wts = []
            if i >= 4:
                wts.append(("sV", mV_st1[i - 4]))
            if slot >= 2:
                wts.append(("sV", mV_ramp))
            mS_xl.append(P.dma("s", xslot[slot],
                               xin[i * 128:(i + 1) * 128, :], waits=tuple(wts),
                               sem=["sXa", "sXb", "sOa", "sOb"][slot]))
            xt = xslot[slot]

            def _bn(E, i=i, xt=xt):
                E.bn_stats(ST[:, BN6:BN6 + 6], xt[:, 0:512])
                E.bn_stats(ST[:, BN6 + 6:BN6 + 12], xt[:, 512:1024])
                return E.bn_aggr(ST[:, MV0 + 2 * i:MV0 + 2 * i + 2], ST[:, BN6:BN6 + 12])
            P.auto("v", _bn, waits=W(mS_xl[i]), n=3)
            mV_st1.append(P.n("sV"))

        mu_all = ST[:, MV0:MV0 + 32:2]
        var_all = ST[:, MV0 + 1:MV0 + 32:2]

        def _lnsc(E):
            E.tensor_scalar(ST[:, CVE:CVE + 16], var_all, 1e-5, None,
                            ALU.add, ALU.bypass)
            return E.tensor_scalar(ST[:, CV2E:CV2E + 16], var_all,
                                   float(np.float32(1.0) + np.float32(1e-5)),
                                   1e-10, ALU.mult, ALU.add)
        P.auto("v", _lnsc, waits=W(("sV", mV_st1[-1])), n=2)
        mV_ln1 = P.n("sV")

        def _lnpw(E):
            E.tensor_tensor(ST[:, CRSTD:CRSTD + 16], ST[:, CVE:CVE + 16],
                            ST[:, CMH:CMH + 16], ALU.pow)
            return E.tensor_tensor(ST[:, CSComb:CSComb + 16], ST[:, CV2E:CV2E + 16],
                                   ST[:, CMH:CMH + 16], ALU.pow)
        P.auto("g", _lnpw, waits=W(("sV", mV_ln1), ("sV", mV_mh)), n=2)
        mP_lnpw = P.n("sP")

        def _lnsc3(E):
            E.tensor_scalar(ST[:, CC1R:CC1R + 16], ST[:, CRSTD:CRSTD + 16], 1.0,
                            None, ALU.add, ALU.bypass)
            return E.tensor_tensor(ST[:, CC0R:CC0R + 16], mu_all,
                                   ST[:, CRSTD:CRSTD + 16], ALU.mult)
        P.auto("v", _lnsc3, waits=W(("sP", mP_lnpw)), n=2)
        mV_lnsc = P.n("sV")

        mrow_l = []

        def _emit_bcast(s_, wts):
            mrow_l.append(P.dma("a", EST if s_ % 2 == 0 else ESTB,
                                _dram_ap(logat_hbm, s_ * 1024, [(0, 128), (1, 1024)]),
                                waits=tuple(wts), sem=["sCza", "sCzb"][s_ % 2]))

        def _emit_eexp(s_):
            dst = EST if s_ % 2 == 0 else ESTB

            def _eexp(E, s_=s_, dst=dst):
                E.activation(Ttby[:, s_ * 1024:s_ * 1024 + 512], dst[:, 0:512],
                             AF.Exp, scale=ST[:, CRV:CRV + 1],
                             accum_out=ST[:, CACC + 2 * s_:CACC + 2 * s_ + 1])
                return E.activation(Ttby[:, s_ * 1024 + 512:(s_ + 1) * 1024],
                                    dst[:, 512:1024], AF.Exp, scale=ST[:, CRV:CRV + 1],
                                    accum_out=ST[:, CACC + 2 * s_ + 1:CACC + 2 * s_ + 2])
            P.auto("a", _eexp, waits=W(mrow_l[s_], ("sV", mV_ramp)), n=2)
            mA_exp.append(P.n("sA"))

        _emit_bcast(0, [mC_lat])
        _emit_bcast(1, [])

        for i in range(TT):
            slot = i % 4
            rslot = i % 2
            _emit_eexp(i)
            if i + 2 < TT:
                _emit_bcast(i + 2, [("sA", mA_exp[i])])
            wts = [("sV", mV_st1[-1])]
            if i >= 4:
                wts.append(("sV", mV_rw[i - 4]))
            mS_x2 = P.dma("s", xslot[slot],
                          xin[i * 128:(i + 1) * 128, :], waits=tuple(wts),
                          sem=["sXa", "sXb", "sOa", "sOb"][slot])
            xt = xslot[slot]
            wts = [mS_x2, ("sV", mV_lnsc)]
            if i >= 4:
                wts.append(("sT", mT_xtr[i - 4]))
            P.auto("v", lambda E, i=i, xt=xt, slot=slot:
                   E.tensor_scalar(xnpslot[slot], xt,
                                   ST[:, MV0 + 2 * i:MV0 + 2 * i + 1],
                                   ST[:, CSComb + i:CSComb + i + 1],
                                   ALU.subtract, ALU.mult),
                   waits=tuple(wts), n=1)
            mV_xnp = P.n("sV")
            wts = []
            if i >= 2:
                wts.append(mC_rs[i - 2])
            P.auto("v", lambda E, i=i, xt=xt, rslot=rslot:
                   E.tensor_scalar(Tob[:, rslot * 1024:(rslot + 1) * 1024], xt,
                                   ST[:, CC1R + i:CC1R + i + 1],
                                   ST[:, CC0R + i:CC0R + i + 1],
                                   ALU.mult, ALU.subtract),
                   waits=tuple(wts), n=1)
            mV_rw.append(P.n("sV"))
            mC_rs.append(P.dma("g", r_hbm.ap()[i * 128:(i + 1) * 128, :],
                               Tob[:, rslot * 1024:(rslot + 1) * 1024],
                               waits=W(("sV", mV_rw[i])),
                               sem=["sCra", "sCrb"][rslot]))

            wts = [("sV", mV_xnp)]
            if i == 0:
                wts.append(mK_pre)
            if i >= 1:
                wts.append(("sA", mA_xev[i - 1][1]))
                wts.append(("sV", mA_xev[i - 1][2]))

            def _tp(E, i=i, slot=slot):
                for fb in range(8):
                    bk = pbtB if fb < 4 else pbtA
                    ins = E.transpose(bk[:, (fb % 4) * 128:(fb % 4 + 1) * 128],
                                      xnpslot[slot][:, fb * 128:(fb + 1) * 128],
                                      identB[:])
                return ins
            P.op("t", _tp, waits=tuple(wts), incs=(("sT", 1),))
            mT_xtr.append(P.n("sT"))

            def _xevA(E, i=i):
                for fb in range(4):
                    ins = E.activation(
                        Txnh[:, fb * 2048 + i * 128:fb * 2048 + (i + 1) * 128],
                        pbtB[:, fb * 128:(fb + 1) * 128], AF.Identity,
                        scale=smallsT[:, 72 + fb:73 + fb],
                        bias=smallsT[:, 80 + fb:81 + fb])
                return ins
            P.auto("a", _xevA, waits=W(("sT", mT_xtr[i]), ("sV", mV_smT)), n=4)
            mA_ev = P.n("sA")

            def _xevV(E, i=i):
                for fb in range(4, 8):
                    ins = E.tensor_scalar(
                        Txnh[:, fb * 2048 + i * 128:fb * 2048 + (i + 1) * 128],
                        pbtA[:, (fb - 4) * 128:(fb - 3) * 128],
                        smallsT[:, 72 + fb:73 + fb], smallsT[:, 80 + fb:81 + fb],
                        ALU.mult, ALU.add)
                return ins
            P.auto("v", _xevV, waits=W(("sT", mT_xtr[i]), ("sV", mV_smT)), n=4)
            mV_ev = P.n("sV")
            mA_xev.append(("both", mA_ev, mV_ev))
        mA_est = P.n("sA")
        mV_est = P.n("sV")
        mP_lnrw = P.n("sP")
        mA_xn_ts = [(("sA", mA_xev[4 * t + 3][1]), ("sV", mA_xev[4 * t + 3][2]))
                    for t in range(4)]

        def _arr(E):
            E.tensor_tensor(ST[:, CAR:CAR + 16], ST[:, CACC:CACC + 32:2],
                            ST[:, CACC + 1:CACC + 32:2], ALU.add)
            return E.tensor_scalar(ST[:, CAR:CAR + 16], ST[:, CAR:CAR + 16],
                                   1.0 / 1024.0, None, ALU.mult, ALU.bypass)
        P.auto("v", _arr, waits=W(("sA", mA_exp[-1])), n=2)
        mV_arr = P.n("sV")
        mC_uhs = []
        for s_ in range(16):
            P.dma("a", _dram_ap(uh_hbm, s_ * 256, [(1, 128), (1, 1)]),
                  ST[:, CAR + s_:CAR + s_ + 1],
                  waits=W(("sV", mV_arr)), sem="sCxp")
            mC_uhs.append(("sCxp", P.n("sCxp")))

        bank_free_a = {}
        mA_xpev = []
        mT_xp = []
        mC_xps = []
        mT_bp = []
        mV_bpev = []
        mM4w = []
        mT_m4 = []
        mA_m4 = []

        def _m4_step(g):
            if g % 2 == 0:
                s_, ph = g // 2, 0
            else:
                if g < 3:
                    return
                s_, ph = (g - 3) // 2, 1
            if s_ > 15:
                return
            if ph == 0:
                wts = [mC_uhs[-1]]
                if s_ < 2:
                    wts.extend([mC_rs[14], mC_rs[15]])
                if s_ >= 2:
                    wts.append(mA_m4[s_ - 2])
                mM4w.append(P.dma("a", M4F[:, (s_ % 2) * 128:(s_ % 2) * 128 + 128],
                                  _dram_ap(uh_hbm, s_ * 256, [(1, 128), (1, 128)]),
                                  waits=tuple(wts), sem=["sM4a", "sM4b"][s_ % 2]))
            else:
                P.auto("a", lambda E, s_=s_:
                       E.activation(MbufB[:, (s_ % 2) * 128:(s_ % 2) * 128 + 128],
                                    M4F[:, (s_ % 2) * 128:(s_ % 2) * 128 + 128],
                                    AF.Identity),
                       waits=W(mM4w[s_]), n=1)
                mcast = P.n("sA")
                wts = [("sA", mcast), ("sA", mA_est)]
                if s_ >= 2:
                    wts.append(("sT", mT_m4[s_ - 2]))
                bank = [pbE, pbF][s_ % 2]
                P.op("t", lambda E, s_=s_, bank=bank:
                     E.matmul(bank[:, 0:128],
                              MbufB[:, (s_ % 2) * 128:(s_ % 2) * 128 + 128],
                              Jbf[:], start=True, stop=True),
                     waits=tuple(wts), incs=(("sT", 1),))
                mT_m4.append(P.n("sT"))
                P.auto("v", lambda E, s_=s_, bank=bank:
                       E.tensor_copy(intra[:, s_ * 128:(s_ + 1) * 128],
                                     bank[:, 0:128]),
                       waits=W(("sT", mT_m4[-1])), n=1)
                mA_m4.append(("sV", P.n("sV")))

        mC_xps_round = []
        it = 0
        for ts_ in range(4):
            for ot in range(8):
                _m4_step(it)
                bank = [pbA, pbB][it % 2]
                wts = [mA_xn_ts[ts_][0], mA_xn_ts[ts_][1], mG_p0]
                if it >= 2:
                    wts.append(("sA", mA_xpev[it - 2]))

                def _mm(E, ot=ot, ts_=ts_, bank=bank):
                    for k in range(8):
                        ins = E.matmul(bank[:, :],
                                       Twb[:, k * 2048 + ot * 128:k * 2048 + (ot + 1) * 128],
                                       Txnh[:, k * 2048 + ts_ * 512:k * 2048 + (ts_ + 1) * 512],
                                       start=(k == 0), stop=(k == 7))
                    return ins
                P.op("t", _mm, waits=tuple(wts), incs=(("sT", 1),))
                mT_xp.append(P.n("sT"))

                wts = [("sT", mT_xp[-1]), ("sV", mV_smT)]
                if ts_ == 0:
                    if ot == 0:
                        wts.append(("sT", mT_xtr[14]))
                    elif ot == 2:
                        wts.append(("sT", mT_xtr[15]))
                    elif ot == 4:
                        wts.append(("sA", mA_est))
                if ts_ >= 1:
                    wts.append(("sT", mT_bp[(ts_ - 1) * 8 + 7]))
                    wts.append(mC_xps_round[ts_ - 1])
                P.auto("a", lambda E, ot=ot, bank=bank:
                     E.activation(Tstrip[:, ot * 512:(ot + 1) * 512], bank[:, :],
                                  AF.Identity, bias=smallsT[:, ot:ot + 1]),
                     waits=tuple(wts), n=1)
                mA_xpev.append(P.n("sA"))

                mC_xps.append(P.dma("a", xp_hbm.ap()[ot * 128:(ot + 1) * 128,
                                                     ts_ * 512:(ts_ + 1) * 512],
                                    Tstrip[:, ot * 512:(ot + 1) * 512],
                                    waits=W(("sA", mA_xpev[-1])), sem="sCxp"))
                it += 1

            wts = [("sA", mA_xpev[-1]), mG_p0]
            if ts_ >= 1:
                wts.append(("sV", mV_bpev[ts_ - 1]))

            def _bp(E, ts_=ts_):
                for k in range(8):
                    ins = E.matmul(pbC[0:16, :], WBTb[:, k * 16:(k + 1) * 16],
                                   Tstrip[:, k * 512:(k + 1) * 512],
                                   start=(k == 0), stop=(k == 7))
                return ins
            P.op("t", _bp, waits=tuple(wts), incs=(("sT", 1),))
            mT_bp.extend([P.n("sT")] * 8)
            P.auto("v", lambda E, ts_=ts_:
                 E.tensor_scalar(B_bf[:, ts_ * 512:(ts_ + 1) * 512], pbC[0:16, :],
                                 ST[0:16, CDTM16:CDTM16 + 1], None, ALU.mult, ALU.bypass),
                 waits=W(("sT", P.n("sT")), ("sV", mV_dtm)), n=1)
            mV_bpev.append(P.n("sV"))
            mC_xps_round.append(("sCxp", P.n("sCxp")))
        _m4_step(33)
        mV_bp = P.n("sV")

        def _btt(E):
            for i in range(16):
                ins = E.transpose(pbtA[:, i * 16:(i + 1) * 16],
                                  B_bf[0:16, i * 128:(i + 1) * 128], identB[0:16, 0:16])
            return ins
        P.op("t", _btt, waits=W(("sV", mV_bp), ("sA", mA_xev[TT - 1][1]),
                                ("sV", mA_xev[TT - 1][2])),
             incs=(("sT", 1),))
        mT_bt = P.n("sT")
        P.auto("v", lambda E: E.tensor_copy(Bt_bf[:], pbtA[:, 0:256]),
             waits=W(("sT", mT_bt)), n=1)
        mV_bt = P.n("sV")

        mV_ssev = []
        mT_cs = []
        for s_ in range(16):
            bank = [pbC, pbD][s_ % 2]
            wts = [("sV", mV_bt), ("sA", mA_est)]
            if s_ >= 2:
                wts.append(("sV", mV_ssev[s_ - 2]))

            def _cs(E, s_=s_, bank=bank):
                for dt_ in range(DT):
                    ins = E.matmul(bank[:, dt_ * 16:(dt_ + 1) * 16],
                                   Ttby[:, s_ * 1024 + dt_ * 128:s_ * 1024 + (dt_ + 1) * 128],
                                   Bt_bf[:, s_::16], start=True, stop=True)
                return ins
            P.op("t", _cs, waits=tuple(wts), incs=(("sT", 1),))
            mT_cs.append(P.n("sT"))
            P.auto("v", lambda E, s_=s_, bank=bank:
                 E.tensor_scalar(
                     Tscr[:, SS0:SS0 + 2048].rearrange("p (i r) -> p i r", r=128)
                     [:, :, s_:s_ + 113:16].rearrange("p i d -> p d i"),
                     bank[:, 0:128].rearrange("p (d i) -> p d i", i=16),
                     1.0 / 1024.0, None, ALU.mult, ALU.bypass),
                 waits=W(("sT", mT_cs[-1])), n=1)
            mV_ssev.append(P.n("sV"))
        mV_ss = P.n("sV")

        def _zit(E):
            E.memset(Tscr[:, ZA0:ZA0 + 128], 0.0)
            Lam = Tscr[:, LI0 + 128:LI0 + 256]
            for i in range(NCH):
                Zi = Tscr[:, ZA0 + i * 128:ZA0 + (i + 1) * 128]
                Zn = Tscr[:, ZA0 + (i + 1) * 128:ZA0 + (i + 2) * 128]
                Si = Tscr[:, SS0 + i * 128:SS0 + (i + 1) * 128]
                E.tensor_tensor(Zn, Zi, Lam, ALU.mult)
                ins = E.tensor_tensor(Zn, Zn, Si, ALU.add)
            return ins
        P.auto("v", _zit, waits=W(("sA", mA_lamI), ("sV", mV_ss), ("sP", mP_lnrw)), n=33)
        mV_z = P.n("sV")

        mT_csdone = P.n("sT")
        _pt_q = []
        for s_ in range(16):
            for dt_ in range(DT):
                _pt_q.append((s_, dt_))

        def _emit_pt(k):
            s_, dt_ = _pt_q[k]
            wts = [("sA", mA_lamI)] if k > 0 else [("sA", mA_lamI), ("sT", mT_csdone)]
            P.auto("a", lambda E, s_=s_, dt_=dt_:
                   E.activation(Ttby[:, s_ * 1024 + dt_ * 128:s_ * 1024 + (dt_ + 1) * 128],
                                RF[:, 0:128], AF.Exp,
                                scale=logA[:, dt_ * 16 + s_:dt_ * 16 + s_ + 1]),
                   waits=tuple(wts), n=1)

        mG_ccin = P.dma("g", cc_in.ap(), Tscr[:, ZA0 + 16 * 128:ZA0 + 17 * 128],
                        waits=W(("sV", mV_z)))
        P.op("g", lambda E: E.collective_compute(
            "AllGather", ALU.bypass, replica_groups=[list(range(NCORES))],
            ins=[cc_in.ap().opt()], outs=[cc_out.ap().opt()]),
            waits=W(mG_ccin), incs=(("sCC", 1),))
        mG_rall = P.dma("g", Tscr[:, RA0:RA0 + 896].rearrange("p (r c) -> p r c", c=128),
                        _dram_ap(cc_out, 0, [(128, 128), (16384, 7), (1, 128)]),
                        waits=W(("sCC", 1),))

        mA_zev = []
        mC_zs = []
        it = 0
        for ot in range(8):
            for ts_ in range(4):
                bank = [pbA, pbB][it % 2]
                wts = []
                if it >= 2:
                    wts.append(mA_zev[it - 2])
                else:
                    wts.append(("sA", mA_xpev[-1]))

                def _mmz(E, ot=ot, ts_=ts_, bank=bank):
                    for k in range(8):
                        ins = E.matmul(bank[:, :],
                                       Twb[:, k * 2048 + (8 + ot) * 128:k * 2048 + (9 + ot) * 128],
                                       Txnh[:, k * 2048 + ts_ * 512:k * 2048 + (ts_ + 1) * 512],
                                       start=(k == 0), stop=(k == 7))
                    return ins
                P.op("t", _mmz, waits=tuple(wts), incs=(("sT", 1),))
                mTz = P.n("sT")
                wts = [("sT", mTz)]
                if it < 4:
                    wts.append(mC_xps_round[3])
                if it >= 2:
                    wts.append(mC_zs[it - 2])
                zslot = it % 4
                P.auto("v", lambda E, zslot=zslot, ot=ot, bank=bank:
                     E.tensor_scalar(Tstrip[:, 2048 + zslot * 512:2048 + (zslot + 1) * 512],
                                     bank[:, :], smallsT[:, 8 + ot:9 + ot], None,
                                     ALU.add, ALU.bypass),
                     waits=tuple(wts), n=1)
                mA_zev.append(("sV", P.n("sV")))
                mC_zs.append(P.dma("a", z_hbm.ap()[ot * 128:(ot + 1) * 128,
                                                    ts_ * 512:(ts_ + 1) * 512],
                                   Tstrip[:, 2048 + zslot * 512:2048 + (zslot + 1) * 512],
                                   waits=W(mA_zev[-1]),
                                   sem=["sCza", "sCzb"][it % 2]))
                for k in range(4 * it, 4 * it + 4):
                    _emit_pt(k)
                it += 1
        mT_zdone = P.n("sT")
        mA_ptdone = P.n("sA")
        mC_zfin = (("sCza", P.n("sCza")), ("sCzb", P.n("sCzb")))

        def _hc(E):
            E.memset(Tscr[:, HC0:HC0 + 128], 0.0)
            Gam = Tscr[:, LI0 + 16 * 128:LI0 + 17 * 128]
            Hc = Tscr[:, HC0:HC0 + 128]
            tmp = Tscr[:, TH0:TH0 + 128]
            for c_ in range(7):
                E.tensor_scalar(tmp, Gam, gmask[:, c_:c_ + 1],
                                gmask[:, 8 + c_:9 + c_], ALU.mult, ALU.add)
                E.tensor_tensor(Hc, Hc, tmp, ALU.mult)
                E.tensor_scalar(tmp, Tscr[:, RA0 + c_ * 128:RA0 + (c_ + 1) * 128],
                                gmask[:, c_:c_ + 1], None, ALU.mult, ALU.bypass)
                ins = E.tensor_tensor(Hc, Hc, tmp, ALU.add)
            return ins
        P.auto("v", _hc, waits=W(mG_rall, ("sA", mA_lamI)), n=29)

        def _hin(E):
            Hc = Tscr[:, HC0:HC0 + 128]
            tmp = Tscr[:, TH0:TH0 + 128]
            for i in range(NCH):
                E.tensor_tensor(tmp, Tscr[:, LI0 + i * 128:LI0 + (i + 1) * 128],
                                Hc, ALU.mult)
                ins = E.tensor_tensor(Tz[:, i * 128:(i + 1) * 128], tmp,
                                      Tscr[:, ZA0 + i * 128:ZA0 + (i + 1) * 128],
                                      ALU.add)
            return ins
        P.auto("v", _hin, n=32)
        mV_hin = P.n("sV")

        mV_stag = []
        for s_ in range(16):
            bank = [pbC, pbD][s_ % 2]
            wts = [("sV", mV_hin), ("sA", mA_ptdone), mA_m4[-1]]
            if s_ >= 2:
                wts.append(("sV", mV_stag[s_ - 2]))

            def _ii(E, s_=s_, bank=bank):
                E.matmul(bank[:, 0:16], intra[:, s_ * 128:(s_ + 1) * 128],
                         Bt_bf[:, s_::16], start=True, stop=False)
                for dt_ in range(DT):
                    ins = E.matmul(bank[:, 0:16],
                                   Ttby[:, s_ * 1024 + dt_ * 128:s_ * 1024 + (dt_ + 1) * 128],
                                   Tz[:, dt_ * 16 + s_::128][:, 0:16],
                                   start=False, stop=(dt_ == DT - 1))
                return ins
            P.op("t", _ii, waits=tuple(wts), incs=(("sT", 1),))
            mTi = P.n("sT")
            P.auto("v", lambda E, s_=s_, bank=bank:
                 E.tensor_copy(stag[:, s_ * 16:(s_ + 1) * 16], bank[:, 0:16]),
                 waits=W(("sT", mTi)), n=1)
            mV_stag.append(P.n("sV"))
        mT_inter_done = P.n("sT")
        BRowb = intra
        P.auto("v", lambda E: E.tensor_copy(BRowb[0:1, 0:2048], BRow[0:1, :]),
               waits=W(("sT", mT_inter_done), mG_brow), n=1)
        mV_browb = P.n("sV")
        mV_stag_done = P.n("sV")

        P.dma("s", stagT[:, 0:128], stag[:, 0:128], waits=W(("sV", mV_stag_done)),
              transpose=True, sem="sSc")
        mS_st = P.dma("s", stagT[:, 128:256], stag[:, 128:256], transpose=True,
                      sem="sSc")
        P.dma("s", xm_hbm.ap()[0:128, :], stagT[:, 0:128], waits=W(mS_st), sem="sSc")
        mS_xm1 = P.dma("s", xm_hbm.ap()[128:256, :], stagT[:, 128:256], sem="sSc")
        mS_xmean = P.dma("s", xmean[:, :],
                         xm_hbm.ap().rearrange("(s r) c -> s (r c)", s=16),
                         waits=W(mS_xm1), sem="sSc")

        for k in range(8):
            P.dma("g", Twb2[:, k * 1024:(k + 1) * 1024],
                  WoutT[k * 128:(k + 1) * 128, :],
                  waits=W(mG_rall) if k == 0 else ())
        mG_wout = ("sG", P.n("sG"))

        mV_y_ts = []
        mV_ydone = []
        mS_rr = []
        mV_x1 = []
        mV_st3 = []
        mC_x1 = []
        mT_wo_tb = []
        ity = 0
        itw = 0
        for ts_ in range(4):
            sem = ["sRa", "sRb"][ts_ % 2]
            wts = []
            if ts_ == 0:
                wts = [mC_xps_round[3], ("sT", mT_inter_done)] + list(mC_zfin)
            else:
                wts = [("sV", mV_y_ts[ts_ - 1])]
            P.dma("s", Tstrip[:].rearrange("p (o c) -> p o c", c=512),
                  _dram_ap(xp_hbm, ts_ * 512,
                           [(2048, 128), (128 * 2048, 8), (1, 512)]),
                  waits=tuple(wts), sem=sem)
            P.dma("s", Tz[:].rearrange("p (o c) -> p o c", c=512),
                  _dram_ap(z_hbm, ts_ * 512,
                           [(2048, 128), (128 * 2048, 8), (1, 512)]),
                  sem=sem)
            mS_grp = (sem, P.n(sem))

            P.auto("a", lambda E: E.activation(Tz[:, 0:4096], Tz[:, 0:4096], AF.Silu),
                   waits=W(mS_grp), n=1)
            mAs = P.n("sA")
            for ot in range(8):
                bank = [pbE, pbF][ity % 2]
                wts = [mS_xmean, mG_p0, mS_grp]
                if ity >= 2:
                    wts.append(("sV", mV_ydone[ity - 2]))
                else:
                    wts.append(mA_m4[-1])
                P.op("t", lambda E, ot=ot, ts_=ts_, bank=bank:
                     E.matmul(bank[:, :], WCTb[:, ot * 128:(ot + 1) * 128],
                              xmean[:, ts_ * 512:(ts_ + 1) * 512],
                              start=True, stop=True),
                     waits=tuple(wts), incs=(("sT", 1),))
                mTy = P.n("sT")

                def _y(E, ot=ot, ts_=ts_, bank=bank):
                    E.scalar_tensor_tensor(
                        Qbuf[:, 0:512], Tstrip[:, ot * 512:(ot + 1) * 512],
                        smallsT[:, 64 + ot:65 + ot], bank[:, :], ALU.mult, ALU.add)
                    return E.tensor_tensor(
                        Ttby[:, ot * 2048 + ts_ * 512:ot * 2048 + (ts_ + 1) * 512],
                        Qbuf[:, 0:512], Tz[:, ot * 512:(ot + 1) * 512], ALU.mult)
                P.auto("v", _y, waits=W(("sT", mTy), ("sA", mAs)), n=2)
                mV_ydone.append(P.n("sV"))
                ity += 1
            mV_y_ts.append(P.n("sV"))

            for tb4 in range(4):
                tb = ts_ * 4 + tb4
                slot = tb % 2
                wts = [mC_rs[14], mC_rs[15]]
                if tb >= 2:
                    wts.append(("sV", mV_x1[(tb - 2) * 2 + 1]))
                else:
                    wts.append(("sV", mV_rw[TT - 1]))
                mS_rr.append(P.dma("s", Tio[:, slot * 1024:(slot + 1) * 1024],
                                   r_hbm.ap()[tb * 128:(tb + 1) * 128, :],
                                   waits=tuple(wts), sem=["sXa", "sXb"][slot]))
                for oh in range(2):
                    bank = [pbA, pbB][itw % 2]
                    wts = [("sV", mV_y_ts[ts_]), mG_wout]
                    if itw >= 2:
                        wts.append(("sV", mV_x1[itw - 2]))
                    else:
                        wts.append(mA_zev[-1])

                    def _wo(E, tb=tb, oh=oh, bank=bank):
                        E.matmul(bank[:, :], onesrow[0:1, :],
                                 BRowb[0:1, oh * 512:(oh + 1) * 512],
                                 start=True, stop=False)
                        for k in range(8):
                            ins = E.matmul(bank[:, :],
                                           Ttby[:, k * 2048 + tb * 128:k * 2048 + (tb + 1) * 128],
                                           Twb2[:, k * 1024 + oh * 512:k * 1024 + (oh + 1) * 512],
                                           start=False, stop=(k == 7))
                        return ins
                    if itw == 0:
                        wts.append(("sV", mV_browb))
                    P.op("t", _wo, waits=tuple(wts), incs=(("sT", 1),))
                    mTw = P.n("sT")
                    P.auto("v", lambda E, tb=tb, oh=oh, slot=slot, bank=bank:
                           E.tensor_tensor(Tob[:, slot * 1024 + oh * 512:slot * 1024 + (oh + 1) * 512],
                                           bank[:, :],
                                           Tio[:, slot * 1024 + oh * 512:slot * 1024 + (oh + 1) * 512],
                                           ALU.add),
                           waits=W(("sT", mTw), mS_rr[tb],
                                   (mC_x1[tb - 2] if tb >= 2 else mC_rs[15]),
                                   (mV_st3[tb - 2] if tb >= 2 else ("sA", 0))),
                           n=1)
                    mV_x1.append(P.n("sV"))
                    itw += 1
                mT_wo_tb.append(P.n("sT"))

                def _bn3(E, tb=tb, slot=slot):
                    ob = Tob[:, slot * 1024:(slot + 1) * 1024]
                    E.activation(Txnh[:, 0:1024], ob, AF.Identity,
                                 accum_out=ST[:, MV3 + 2 * tb:MV3 + 2 * tb + 1])
                    return E.activation(Txnh[:, 0:1024], ob, AF.Square,
                                        accum_out=ST[:, MV3 + 2 * tb + 1:MV3 + 2 * tb + 2])
                P.auto("a", _bn3, waits=W(("sV", mV_x1[-1])), n=2)
                mV_st3.append(("sA", P.n("sA")))
                mC_x1.append(P.dma("a", x1_hbm.ap()[tb * 128:(tb + 1) * 128, :],
                                   Tob[:, slot * 1024:(slot + 1) * 1024],
                                   waits=W(("sV", mV_x1[-1])),
                                   sem=["sC1a", "sC1b"][slot]))
        mT_woutdone = P.n("sT")
        mV_y = P.n("sV")

        mV_st3d = mV_st3[-1]
        def _ln3s(E):
            E.tensor_scalar(ST[:, MV3:MV3 + 32:2], ST[:, MV3:MV3 + 32:2],
                            1.0 / 1024.0, None, ALU.mult, ALU.bypass)
            E.tensor_tensor(ST[:, CLN:CLN + 16], ST[:, MV3:MV3 + 32:2],
                            ST[:, MV3:MV3 + 32:2], ALU.mult)
            E.tensor_scalar(ST[:, CV3E:CV3E + 16], ST[:, MV3 + 1:MV3 + 32:2],
                            1.0 / 1024.0, 1e-5, ALU.mult, ALU.add)
            return E.tensor_tensor(ST[:, CV3E:CV3E + 16], ST[:, CV3E:CV3E + 16],
                                   ST[:, CLN:CLN + 16], ALU.subtract)
        P.auto("v", _ln3s, waits=W(mV_st3d), n=4)
        P.auto("g", lambda E: E.tensor_tensor(ST[:, CRS3:CRS3 + 16],
                                              ST[:, CV3E:CV3E + 16],
                                              ST[:, CMH:CMH + 16], ALU.pow),
               waits=W(("sV", P.n("sV"))), n=1)
        mA_rs3 = ("sP", P.n("sP"))

        mS_l2, mV_n3, mT_t3, mV_e3 = [], [], [], []
        for i in range(TT):
            slot = i % 2
            wts = [mC_x1[i]]
            if i >= 2:
                wts.append(("sV", mV_n3[i - 2]))
            else:
                wts.append(("sV", mV_x1[-1]))
            mS_l2.append(P.dma("s", Tio[:, slot * 1024:(slot + 1) * 1024],
                               x1_hbm.ap()[i * 128:(i + 1) * 128, :],
                               waits=tuple(wts), sem=["sXa", "sXb"][slot]))
            wts = [mS_l2[i], mA_rs3, ("sT", mT_wo_tb[i])]
            if i >= 2:
                wts.append(("sT", mT_t3[i - 2]))
            P.auto("v", lambda E, i=i, slot=slot:
                 E.tensor_scalar(xnp[:, slot * 1024:(slot + 1) * 1024],
                                 Tio[:, slot * 1024:(slot + 1) * 1024],
                                 ST[:, MV3 + 2 * i:MV3 + 2 * i + 1],
                                 ST[:, CRS3 + i:CRS3 + i + 1],
                                 ALU.subtract, ALU.mult),
                 waits=tuple(wts), n=1)
            mV_n3.append(P.n("sV"))
            bank = [pbtA, pbtB][i % 2]
            wts = [("sV", mV_n3[i])]
            if i >= 2:
                wts.append(("sV", mV_e3[i - 2]))

            def _tp3(E, i=i, slot=slot, bank=bank):
                for fb in range(8):
                    ins = E.transpose(bank[:, fb * 128:(fb + 1) * 128],
                                      xnp[:, slot * 1024 + fb * 128:slot * 1024 + (fb + 1) * 128],
                                      identB[:])
                return ins
            P.op("t", _tp3, waits=tuple(wts), incs=(("sT", 1),))
            mT_t3.append(P.n("sT"))
            P.auto("v", lambda E, i=i, bank=bank:
                 E.tensor_copy(Ttby[:].rearrange("p (a b) -> p a b", b=2048)
                               [:, :, i * 128:(i + 1) * 128],
                               bank[:, 0:1024].rearrange("p (a b) -> p a b", b=128)),
                 waits=W(("sT", mT_t3[i])), n=1)
            mV_e3.append(P.n("sV"))
        mV_x1n = P.n("sV")

        for k in range(8):
            P.dma("g", Twb[:, k * 256:(k + 1) * 256], U1d[k * 128:(k + 1) * 128, :],
                  waits=W(("sT", mT_zdone), mG_wout) if k == 0 else ())
        for k in range(2):
            P.dma("g", Twb[:, 2048 + k * 4096:2048 + (k + 1) * 4096],
                  V1d[k * 128:(k + 1) * 128, :])
        for k in range(2):
            P.dma("g", Twb[:, 10240 + k * 1024:10240 + (k + 1) * 1024],
                  V2d[k * 128:(k + 1) * 128, :])
        mG_ffw1 = ("sG", P.n("sG"))
        for k in range(32):
            P.dma("g", Twb2[:, k * 256:(k + 1) * 256], U2d[k * 128:(k + 1) * 128, :],
                  waits=W(("sT", mT_woutdone), mG_ffw1) if k == 0 else ())
        mG_ffw2 = ("sG", P.n("sG"))

        PB0 = 12288
        mV_pev = []
        mA_hev = []
        mT_u2 = []
        mV_qev = []
        mV_oev = []
        mS_ol = []
        mS_os = []
        for st in range(4):
            for rt in range(2):
                k2 = st * 2 + rt
                bank = [pbC, pbD][rt]
                wts = [("sV", mV_e3[4 * st + 3]), mG_ffw1]
                if st >= 1:
                    wts.append(("sV", mV_qev[(st - 1) * 2 + rt]))

                def _u1(E, st=st, rt=rt, bank=bank):
                    for k in range(8):
                        ins = E.matmul(bank[:, :],
                                       Twb[:, k * 256 + rt * 128:k * 256 + (rt + 1) * 128],
                                       Ttby[:, k * 2048 + st * 512:k * 2048 + (st + 1) * 512],
                                       start=(k == 0), stop=(k == 7))
                    return ins
                P.op("t", _u1, waits=tuple(wts), incs=(("sT", 1),))
                mTp = P.n("sT")
                P.auto("v", lambda E, st=st, rt=rt, bank=bank:
                       E.tensor_copy(Twb[:, PB0 + rt * 2048 + st * 512:PB0 + rt * 2048 + (st + 1) * 512],
                                     bank[:, :]),
                       waits=W(("sT", mTp)), n=1)
                mV_pev.append(P.n("sV"))
            mV_pst = P.n("sV")

            def emit_u2(ot):
                wts = [("sA", mA_hev[st * 32 + ot])]
                if st >= 1 and ot == 0:
                    pass

                def _u2(E, st=st, ot=ot):
                    sl = (st * 32 + ot) % 3
                    E.matmul(pbC[:, :], Twb2[:, ot * 256:ot * 256 + 128],
                             Txnh[:, sl * 512:(sl + 1) * 512],
                             start=(ot == 0), stop=(ot == 31))
                    return E.matmul(pbD[:, :], Twb2[:, ot * 256 + 128:ot * 256 + 256],
                                    Txnh[:, sl * 512:(sl + 1) * 512],
                                    start=(ot == 0), stop=(ot == 31))
                P.op("t", _u2, waits=tuple(wts), incs=(("sT", 1),))
                mT_u2.append(P.n("sT"))

            for ot in range(32):
                g = st * 32 + ot
                bank = [pbA, pbB][ot % 2]
                wts = [("sV", mV_pst), mG_ffw2]
                if g >= 2:
                    wts.append(("sA", mA_hev[g - 2]))

                def _v1(E, st=st, ot=ot, bank=bank):
                    for k in range(2):
                        ins = E.matmul(bank[:, :],
                                       Twb[:, 2048 + k * 4096 + ot * 128:2048 + k * 4096 + (ot + 1) * 128],
                                       Twb[:, PB0 + k * 2048 + st * 512:PB0 + k * 2048 + (st + 1) * 512],
                                       start=(k == 0), stop=(k == 1))
                    return ins
                P.op("t", _v1, waits=tuple(wts), incs=(("sT", 1),))
                mTh = P.n("sT")
                wts = [("sT", mTh), ("sV", mV_smT)]
                if g >= 3:
                    wts.append(("sT", mT_u2[g - 3]))
                P.auto("a", lambda E, ot=ot, g=g, bank=bank:
                       E.activation(Txnh[:, (g % 3) * 512:(g % 3 + 1) * 512],
                                    bank[:, :], AF.Gelu_apprx_tanh,
                                    bias=smallsT[:, 24 + ot:25 + ot]),
                       waits=tuple(wts), n=1)
                mA_hev.append(P.n("sA"))
                if ot >= 2:
                    emit_u2(ot - 2)
            emit_u2(30)
            emit_u2(31)

            for rt2 in range(2):
                bank = [pbC, pbD][rt2]
                wts = [("sT", mT_u2[-1])]
                if st >= 1:
                    wts.append(("sT", mT_v2last))
                P.auto("v", lambda E, rt2=rt2, bank=bank:
                       E.tensor_copy(Qbuf[:, rt2 * 512:(rt2 + 1) * 512], bank[:, :]),
                       waits=tuple(wts), n=1)
                mV_qev.append(P.n("sV"))
            mV_qst = P.n("sV")

            for tb4 in range(4):
                tb = st * 4 + tb4
                slot = tb % 2
                wts = [mC_x1[tb]]
                if tb >= 2:
                    wts.append(("sV", mV_oev[(tb - 2) * 2 + 1]))
                else:
                    wts.append(("sV", mV_n3[TT - 1]))
                mS_ol.append(P.dma("s", Tio[:, slot * 1024:(slot + 1) * 1024],
                                   x1_hbm.ap()[tb * 128:(tb + 1) * 128, :],
                                   waits=tuple(wts), sem=["sXa", "sXb"][slot]))
                for dh in range(2):
                    bank = [pbE, pbF][dh]
                    wts = [("sV", mV_qst)]
                    if tb >= 1:
                        wts.append(("sV", mV_oev[(tb - 1) * 2 + dh]))

                    def _v2(E, tb4=tb4, dh=dh, bank=bank):
                        E.matmul(bank[:, :], onesrow[0:1, :],
                                 BRowb[0:1, 1024 + dh * 512:1024 + (dh + 1) * 512],
                                 start=True, stop=False)
                        for k in range(2):
                            ins = E.matmul(bank[:, :],
                                           Qbuf[:, k * 512 + tb4 * 128:k * 512 + (tb4 + 1) * 128],
                                           Twb[:, 10240 + k * 1024 + dh * 512:10240 + k * 1024 + (dh + 1) * 512],
                                           start=False, stop=(k == 1))
                        return ins
                    P.op("t", _v2, waits=tuple(wts), incs=(("sT", 1),))
                    mTo = P.n("sT")
                    wts = [("sT", mTo), mS_ol[tb]]
                    if tb >= 2:
                        wts.append(mS_os[tb - 2])
                    else:
                        wts.extend([mC_x1[14], mC_x1[15], mV_st3[-1]])
                    P.auto("v", lambda E, slot=slot, dh=dh, bank=bank:
                           E.tensor_tensor(
                               Tob[:, slot * 1024 + dh * 512:slot * 1024 + (dh + 1) * 512],
                               bank[:, :],
                               Tio[:, slot * 1024 + dh * 512:slot * 1024 + (dh + 1) * 512],
                               ALU.add),
                           waits=tuple(wts), n=1)
                    mV_oev.append(P.n("sV"))
                mS_os.append(P.dma("g", yout[tb * 128:(tb + 1) * 128, :],
                                   Tob[:, slot * 1024:(slot + 1) * 1024],
                                   waits=W(("sV", mV_oev[-1])),
                                   sem=["sGoa", "sGob"][slot]))
            mT_v2last = P.n("sT")

        allsems = ("sG", "sXa", "sXb", "sOa", "sOb", "sRa", "sRb", "sSc",
                   "sCra", "sCrb", "sCxp", "sCza", "sCzb", "sC1a", "sC1b",
                   "sM4a", "sM4b", "sGoa", "sGob", "sYp", "sK", "sT", "sV", "sA")
        P.op("g", lambda E: E.engine_nop(),
             waits=tuple((s, P.n(s)) for s in allsems),
             incs=(("sP", 1),))

        P.emit(blk, sems)

    return nc


_CACHE = {}


def _host_inputs(inputs):
    f32 = np.float32
    W_in = np.asarray(inputs["L0_attn_W_in"], f32)
    smalls = np.concatenate([
        np.asarray(inputs["L0_attn_b_in"], f32).reshape(16, 128),
        np.asarray(inputs["L0_attn_b_out"], f32).reshape(8, 128),
        np.asarray(inputs["L0_ffn_b1"], f32).reshape(32, 128),
        np.asarray(inputs["L0_ffn_b2"], f32).reshape(8, 128),
        np.asarray(inputs["L0_attn_D_skip"], f32).reshape(8, 128),
        np.asarray(inputs["L0_attn_ln_g"], f32).reshape(8, 128),
        np.asarray(inputs["L0_attn_ln_b"], f32).reshape(8, 128),
        np.asarray(inputs["L0_attn_log_dt"], f32).reshape(8, 128),
    ], axis=0)
    brow = np.concatenate([np.asarray(inputs["L0_attn_b_out"], f32),
                           np.asarray(inputs["L0_ffn_b2"], f32)]).reshape(1, 2048)
    crow = np.zeros((1, 256), f32)
    crow[0, 0:128] = np.arange(127, -1, -1, dtype=f32)
    crow[0, 128:256] = 1.0
    com = {
        "WinT": np.ascontiguousarray(W_in.T),
        "WBT": np.ascontiguousarray(np.asarray(inputs["L0_attn_W_B"], f32).T),
        "WCT": np.ascontiguousarray(np.asarray(inputs["L0_attn_W_C"], f32).T),
        "WoutT": np.ascontiguousarray(np.asarray(inputs["L0_attn_W_out"], f32).T),
        "U1": np.ascontiguousarray(np.asarray(inputs["L0_ffn_U1"], f32)),
        "V1": np.ascontiguousarray(np.asarray(inputs["L0_ffn_V1"], f32)),
        "U2": np.ascontiguousarray(np.asarray(inputs["L0_ffn_U2"], f32)),
        "V2": np.ascontiguousarray(np.asarray(inputs["L0_ffn_V2"], f32)),
        "smalls": smalls,
        "Alog": np.ascontiguousarray(np.asarray(inputs["L0_attn_A_log"], f32)),
        "brow": brow,
        "identF": np.eye(128, dtype=f32),
        "identB": np.eye(128, dtype=NPBF),
        "Jbf": np.eye(128, dtype=NPBF)[:, ::-1].copy(),
        "onesrow": np.ones((1, 128), NPBF),
        "crow": crow,
        "ccol": np.ones((128, 1), f32),
        "conesb": np.ones((128, 2), NPBF),
    }
    x = np.asarray(inputs["L0_attn_x"], f32).reshape(16384, 1024)
    in_maps = []
    for c in range(NCORES):
        m = dict(com)
        m["xin"] = np.ascontiguousarray(x[c * LCORE:(c + 1) * LCORE])
        gm = np.zeros((128, 16), f32)
        gm[:, 0:7] = (np.arange(7)[None, :] < c)
        gm[:, 8:15] = 1.0 - gm[:, 0:7]
        m["gmask"] = gm
        in_maps.append(m)
    return in_maps


def kernel(**inputs):
    if "nc" not in _CACHE:
        _CACHE["nc"] = build_nc()
    nc = _CACHE["nc"]
    in_maps = _host_inputs(inputs)
    res = run_bass_kernel_spmd(nc, in_maps, core_ids=list(range(NCORES)))
    out = np.concatenate([np.asarray(res.results[c]["y"]) for c in range(NCORES)],
                         axis=0)
    return out.reshape(1, 16384, 1024).astype(np.float32)
